# version 22
# baseline (speedup 1.0000x reference)
import math
from contextlib import ExitStack

import numpy as np

import concourse.bass as bass
import concourse.mybir as mybir
from concourse.alu_op_type import AluOpType as OP
from concourse.bass_utils import run_bass_kernel_spmd
from concourse.bass_core_rust import IndirectOffsetOnAxis

F32 = mybir.dt.float32
BF16 = mybir.dt.bfloat16
AF = mybir.ActivationFunctionType
AX = mybir.AxisListType

T = 2048
DM = 1024
NT = 16
ND = 8
NP = 8
NK = 16
NE = 18
LAMBDA_INIT = 0.8 - 0.6 * math.exp(-0.3)
LN_EPS = 1e-5
RMS_BIAS = math.log(1.0 - LAMBDA_INIT)
ATTN_PAIRS = NP
CAP = 768
DUMP = 16 * CAP


def _host_consts():
    D = 64
    inv_freq = 1.0 / 10000.0 ** (np.arange(0, D, 2, dtype=np.float64) / D)
    pos = np.arange(T, dtype=np.float64)
    freqs = pos[:, None] * inv_freq[None, :]
    freqs = np.repeat(freqs, 2, axis=-1)
    sv = (np.arange(0, D, 2, dtype=np.float64) + 0.4 * D) / (1.4 * D)
    power = (pos - T // 2) / 512.0
    scale = sv[None, :] ** power[:, None]
    scale = np.concatenate([scale, scale], axis=-1)

    def fm(m):
        return np.tile(m.T, (2, 1)).astype(np.float32)

    aq = fm(np.cos(freqs) * scale / 8.0)
    bq = fm(np.sin(freqs) * scale / 8.0)
    ak = fm(np.cos(freqs) / scale)
    bk = fm(np.sin(freqs) / scale)
    m = np.zeros((128, 128), dtype=np.float32)
    for blk in (0, 64):
        for a in range(32):
            m[blk + 2 * a, blk + 2 * a + 1] = -1.0
            m[blk + 2 * a + 1, blk + 2 * a] = 1.0
    tokid = np.zeros((128, 16), dtype=np.float32)
    for t in range(16):
        tokid[:, t] = t * 128 + np.arange(128)
    eoff = np.zeros((128, 1), dtype=np.float32)
    for e in range(16):
        eoff[e, 0] = e * 6
    return {"aq": aq, "bq": bq, "ak": ak, "bk": bk, "permT": m.T.copy(),
            "ident": np.eye(128, dtype=np.float32),
            "tokid": tokid, "eoff": eoff}


def build_program(stage=4, taps=()):
    nc = bass.Bass("TRN2", target_bir_lowering=False)
    taps = set(taps)

    def din(name, shape):
        return nc.dram_tensor(name, list(shape), F32, kind="ExternalInput").ap()

    x_d = din("x", (T, DM))
    wq_d = din("wq", (DM, DM))
    wk_d = din("wk", (DM, DM))
    wv_d = din("wv", (DM, DM))
    wo_d = din("wo", (DM, DM))
    lam_d = [din(f"lam{i}", (64,)) for i in range(4)]
    rw_d = din("rw", (16, DM))
    rbias_d = din("rbias", (16,))
    gw_d = din("gw", (16, DM, DM))
    uw_d = din("uw", (16, DM, DM))
    dw_d = din("dw", (16, DM, DM))
    shg_d = din("shg", (DM, 2048))
    shu_d = din("shu", (DM, 2048))
    shd_d = din("shd", (2048, DM))
    aq_d = din("aq", (128, T))
    bq_d = din("bq", (128, T))
    ak_d = din("ak", (128, T))
    bk_d = din("bk", (128, T))
    perm_d = din("perm", (128, 128))
    ident_d = din("ident", (128, 128))
    tokid_d = din("tokid", (128, 16))
    eoff_d = din("eoff", (128, 1))
    out_d = nc.dram_tensor("out", [T, DM], F32, kind="ExternalOutput").ap()
    crs_hi = nc.dram_tensor("crs_hi", [32, 128], BF16, kind="Internal").ap()
    crs_lo = nc.dram_tensor("crs_lo", [32, 128], BF16, kind="Internal").ap()
    x1t_d = nc.dram_tensor("x1t", [T + 1, DM], BF16, kind="Internal").ap()
    cmbd_d = nc.dram_tensor("cmbd", [T + 1, 16], F32, kind="Internal").ap()
    slotl_d = nc.dram_tensor("slotl", [DUMP + 1, 1], F32, kind="Internal").ap()
    yrank_d = nc.dram_tensor("yrank", [3 * 4096 + T, DM], F32,
                             kind="Internal").ap()
    gwb_d = nc.dram_tensor("gwb", [16, DM, DM], BF16, kind="Internal").ap()
    uwb_d = nc.dram_tensor("uwb", [16, DM, DM], BF16, kind="Internal").ap()
    dwb_d = nc.dram_tensor("dwb", [16, DM, DM], BF16, kind="Internal").ap()
    shgb_d = nc.dram_tensor("shgb", [DM, 2048], BF16, kind="Internal").ap()
    shub_d = nc.dram_tensor("shub", [DM, 2048], BF16, kind="Internal").ap()
    shdb_d = nc.dram_tensor("shdb", [2048, DM], BF16, kind="Internal").ap()
    xb_d = nc.dram_tensor("xb", [T, DM], BF16, kind="Internal").ap()
    wqb2_d = nc.dram_tensor("wqb2", [DM, DM], BF16, kind="Internal").ap()
    wkb2_d = nc.dram_tensor("wkb2", [DM, DM], BF16, kind="Internal").ap()
    wvb2_d = nc.dram_tensor("wvb2", [DM, DM], BF16, kind="Internal").ap()
    wob2_d = nc.dram_tensor("wob2", [DM, DM], BF16, kind="Internal").ap()

    tap_shapes = {"qall": (128, NP * T), "kall": (128, NP * T),
                  "v": (128, NT * DM), "xfm": (128, ND * T),
                  "attnfm": (128, NP * T), "acc": (128, NT * DM),
                  "x1fm": (128, ND * T), "cmb": (128, NT * 16),
                  "crow": (1, 2 * T), "crowlo": (1, 2 * T), "sa": (128, 32), "g0": (128, ND * CAP),
                  "ut": (128, T), "ptb1": (128, T), "nmx": (128, 32),
                  "lnS": (128, 32), "chi": (128, 32), "clo": (128, 32),
                  "thi": (32, 128), "tlo": (32, 128), "cf": (128, 32),
                  "postm": (128, 256), "slotl": (DUMP + 1, 1),
                  "accd": (T + 1, DM), "selst": (128, 256)}
    tap_d = {}
    for t_ in taps:
        dt_ = F32 if t_ in ("acc", "cmb", "sa", "ut", "nmx", "lnS", "cf",
                            "postm", "slotl", "accd", "selst") else BF16
        tap_d[t_] = nc.dram_tensor("tap_" + t_, list(tap_shapes[t_]), dt_,
                                   kind="ExternalOutput").ap()

    with ExitStack() as S:
        sb = lambda name, shape, dt, **kw: S.enter_context(
            nc.sbuf_tensor(name, list(shape), dt, **kw))
        sem = lambda name: S.enter_context(nc.semaphore(name))

        permT_b = sb("permT", (128, 128), BF16)
        ident_b = sb("identb", (128, 128), BF16)
        onesrow_bf = sb("onesrow_bf", (1, 128), BF16)
        onescol_bf = sb("onescol_bf", (128, 1), BF16)
        onesrow_f = sb("onesrow_f", (1, 128), F32)
        ones4_f = sb("ones4_f", (128, 4), F32)
        neglam = sb("neglam", (128, 1), F32)
        routerT = sb("routerT", (128, 128), BF16)
        biasbc = sb("biasbc", (128, 16), F32)
        lamrow = sb("lamrow", (1, 256), F32)
        lsc = sb("lsc", (1, 96), F32)
        cmb = sb("cmb", (128, NT * 16), F32)
        constb = sb("constb", (128, 2), F32)
        SR_ = S.enter_context(ExitStack())
        fmbuf = SR_.enter_context(
            nc.sbuf_tensor("fmbuf", [128, ND * T], BF16, side="right"))
        woT = SR_.enter_context(
            nc.sbuf_tensor("woT", [128, ND * DM], BF16, side="right"))

        ldx = sem("ldx")
        ldpi = sem("ldpi")
        ldwo = sem("ldwo")
        ldrt = sem("ldrt")
        ldw = sem("ldw")
        ldtab = sem("ldtab")
        ldmisc = sem("ldmisc")
        a_pe = sem("a_pe")
        a_dve = sem("a_dve")
        a_act = sem("a_act")
        b_pe = sem("b_pe")
        b_dve = sem("b_dve")
        b_pex = sem("b_pex")
        b_dvex = sem("b_dvex")
        b_peQ = sem("b_peQ")
        b_actQ = sem("b_actQ")
        b_peR = sem("b_peR")
        b_dveQ = sem("b_dveQ")
        v_pe = sem("v_pe")
        v_act = sem("v_act")
        c_peS = sem("c_peS")
        c_max = sem("c_max")
        c_s1a = sem("c_s1a")
        c_s1b = sem("c_s1b")
        c_s1c = sem("c_s1c")
        c_s1d = sem("c_s1d")
        c_sumc = sem("c_sumc")
        c_s2a = sem("c_s2a")
        c_s2b = sem("c_s2b")
        c_epi = sem("c_epi")
        c_lnS = sem("c_lnS")
        c_crow = sem("c_crow")
        c_ctr = sem("c_ctr")
        c_scr = sem("c_scr")
        c_cmat = sem("c_cmat")
        c_peCT = sem("c_peCT")
        c_aug = sem("c_aug")
        c_crowD = sem("c_crowD")
        c_peST = sem("c_peST")
        c_pv = sem("c_pv")
        c_ep1 = sem("c_ep1")
        c_ep2 = sem("c_ep2")
        c_ep3 = sem("c_ep3")
        c_ep4 = sem("c_ep4")
        c_attnD = sem("c_attnD")
        d_xld = [sem("d_xld0"), sem("d_xld1")]
        d_peO = sem("d_peO")
        d_add = sem("d_add")
        d_bn = sem("d_bn")
        d_act = sem("d_act")
        d_x1b = sem("d_x1b")
        d_peT = sem("d_peT")
        d_evT = sem("d_evT")
        r_pe = sem("r_pe")
        r_act = sem("r_act")
        r_dve = sem("r_dve")
        e_wld = [sem(f"e_wld{i}") for i in range(4)]
        e_wdone = sem("e_wdone")
        e_peH = sem("e_peH")
        e_peU = sem("e_peU")
        e_silu = sem("e_silu")
        e_gmul = sem("e_gmul")
        e_peY = sem("e_peY")
        e_yev = sem("e_yev")
        sp_z = sem("sp_z")
        sp_wcast = sem("sp_wcast")
        sp_pcast = sem("sp_pcast")
        sp_zacc = sem("sp_zacc")
        sp_cmb = sem("sp_cmb")
        sp_dmisc = sem("sp_dmisc")
        sp_x1d = [sem("sp_x1d0"), sem("sp_x1d1")]
        d_selT = sem("d_selT")
        d_selE = sem("d_selE")
        d_pos = sem("d_pos")
        d_posT = sem("d_posT")
        d_posE = sem("d_posE")
        sp_scat = sem("sp_scat")
        sp_slotl = sem("sp_slotl")
        d_slotc = sem("d_slotc")
        sp_xg = sem("sp_xg")
        sp_xts = sem("sp_xts")
        sp_gdone = sem("sp_gdone")
        sp_wg = sem("sp_wg")
        sp_sadd = sem("sp_sadd")
        f_dve = sem("f_dve")
        f_act = sem("f_act")
        f_out = [sem("f_out0"), sem("f_out1")]
        f_ld = [sem("f_ld0"), sem("f_ld1")]
        f_addc = sem("f_addc")
        tapsem = sem("tapsem")
        tap_count = [0]

        def emit_tap(name, ap):
            if name not in taps:
                return
            with nc.Block() as blk:
                def body(sy, ap=ap, name=name):
                    sy.dma_start(tap_d[name][:, :], ap).then_inc(tapsem, 16)
                    tap_count[0] += 16
                    sy.wait_ge(tapsem, tap_count[0])
                blk.sync(body)

        S1 = S.enter_context(ExitStack())
        sb1 = lambda name, shape, dt: S1.enter_context(
            nc.sbuf_tensor(name, list(shape), dt))
        vbuf = sb1("vbuf", (128, NT * DM), BF16)
        qall = sb1("qall", (128, NP * T), BF16)
        kall = sb1("kall", (128, NP * T), BF16)
        S1w = S.enter_context(ExitStack())
        wstage = S1w.enter_context(
            nc.sbuf_tensor("wstage", [128, ND * DM], BF16))

        with ExitStack() as SA:
            rstage = SA.enter_context(nc.sbuf_tensor("rstage", [16, DM], BF16))
            pbf = SA.enter_context(nc.psum_tensor("pbf", [128, DM], BF16))
            psf = SA.enter_context(nc.psum_tensor("psf", [128, 512], F32))
            with nc.Block() as blk:
                def gp(g):
                    def flat(ap):
                        return ap.rearrange("a b -> (a b)").unsqueeze(0)
                    for src, dst in ((wo_d, wob2_d), (wq_d, wqb2_d),
                                     (x_d, xb_d), (wk_d, wkb2_d),
                                     (wv_d, wvb2_d)):
                        g.dma_start(flat(dst[:, :]), flat(src[:, :])
                                    ).then_inc(sp_pcast, 16)
                    g.dma_start(permT_b[:, :], perm_d[:, :]).then_inc(ldpi, 16)
                    g.dma_start(ident_b[:, :], ident_d[:, :]).then_inc(ldpi, 16)
                    g.dma_start(rstage[:, :], rw_d[:, :]).then_inc(ldrt, 16)
                blk.gpsimd(gp)

                def sy(s):
                    for i in range(4):
                        s.dma_start(lamrow[0:1, i * 64:(i + 1) * 64],
                                    lam_d[i].unsqueeze(0)).then_inc(ldmisc, 16)
                    s.dma_start(lsc[0:1, 80:96],
                                rbias_d.unsqueeze(0)).then_inc(ldmisc, 16)
                    w3d = wstage[:, :].rearrange("p (c f) -> p c f", c=ND)
                    s.wait_ge(sp_pcast, 16)
                    s.dma_start(w3d, wob2_d.rearrange("(c p) f -> p c f",
                                                      p=128)).then_inc(ldwo, 16)
                    s.wait_ge(a_pe, 8)
                    s.wait_ge(sp_pcast, 32)
                    s.dma_start(w3d, wqb2_d.rearrange("(c p) f -> p c f",
                                                      p=128)).then_inc(ldw, 16)
                    s.wait_ge(sp_pcast, 48)
                    x3s = xb_d.rearrange("(c p) f -> p c f", p=128)
                    x3d = vbuf[:, :].rearrange("p (c f) -> p c f", c=NT)
                    s.dma_start(x3d, x3s).then_inc(ldx, 16)
                blk.sync(sy)

                def pe(t):
                    t.wait_ge(ldwo, 16)
                    t.wait_ge(ldpi, 32)
                    for dc in range(ND):
                        if dc >= 1:
                            t.wait_ge(a_dve, dc)
                        last = None
                        for ic in range(ND):
                            last = t.transpose(
                                pbf[:, ic * 128:(ic + 1) * 128],
                                wstage[:, ic * DM + dc * 128:
                                       ic * DM + dc * 128 + 128],
                                ident_b[:, :])
                        last.then_inc(a_pe, 1)
                    t.wait_ge(ldrt, 16)
                    t.wait_ge(a_dve, 8)
                    last = None
                    for dc in range(ND):
                        last = t.transpose(pbf[:, dc * 16:dc * 16 + 16],
                                           rstage[0:16, dc * 128:(dc + 1) * 128],
                                           ident_b[0:16, 0:16])
                    last.then_inc(a_pe, 1)
                    t.wait_ge(a_dve, 11)
                    t.matmul(psf[:, 0:1], onesrow_f[0:1, :], lsc[0:1, 68:69],
                             start=True, stop=True).then_inc(a_pe, 1)
                    t.wait_ge(a_dve, 12)
                    t.matmul(psf[:, 0:16], onesrow_f[0:1, :], lsc[0:1, 80:96],
                             start=True, stop=True).then_inc(a_pe, 1)
                blk.tensor(pe)

                def ve(v):
                    v.memset(onesrow_bf[:, :], 1.0)
                    v.memset(onescol_bf[:, :], 1.0)
                    v.memset(onesrow_f[:, :], 1.0)
                    v.memset(ones4_f[:, :], 1.0)
                    v.memset(constb[:, 0:1], LN_EPS)
                    v.memset(constb[:, 1:2], RMS_BIAS)
                    for dc in range(ND):
                        v.wait_ge(a_pe, dc + 1)
                        v.tensor_copy(woT[:, dc * DM:(dc + 1) * DM],
                                      pbf[:, :]).then_inc(a_dve, 1)
                    v.wait_ge(a_pe, 9)
                    v.tensor_copy(routerT[:, :], pbf[:, 0:128]).then_inc(a_dve, 1)
                    v.wait_ge(ldmisc, 80)
                    v.tensor_mul(lsc[0:1, 0:64], lamrow[0:1, 0:64],
                                 lamrow[0:1, 64:128])
                    v.reduce_sum(lsc[0:1, 64:65], lsc[0:1, 0:64], axis=AX.X)
                    v.tensor_mul(lsc[0:1, 0:64], lamrow[0:1, 128:192],
                                 lamrow[0:1, 192:256])
                    v.reduce_sum(lsc[0:1, 65:66], lsc[0:1, 0:64],
                                 axis=AX.X).then_inc(a_dve, 1)
                    v.wait_ge(a_act, 1)
                    v.tensor_sub(lsc[0:1, 68:69], lsc[0:1, 66:67], lsc[0:1, 67:68])
                    v.drain()
                    v.tensor_scalar_add(lsc[0:1, 68:69], lsc[0:1, 68:69],
                                        LAMBDA_INIT).then_inc(a_dve, 1)
                    v.wait_ge(a_pe, 10)
                    v.tensor_scalar_mul(neglam[:, :], psf[:, 0:1],
                                        -1.0).then_inc(a_dve, 1)
                    v.wait_ge(a_pe, 11)
                    v.tensor_copy(biasbc[:, :], psf[:, 0:16])
                blk.vector(ve)

                def ac(s):
                    s.wait_ge(a_dve, 10)
                    s.activation(lsc[0:1, 66:68], lsc[0:1, 64:66],
                                 AF.Exp).then_inc(a_act, 1)
                blk.scalar(ac)

        with ExitStack() as SB:
            pxf = [SB.enter_context(nc.psum_tensor(f"pxf{i}", [128, DM], BF16))
                   for i in range(2)]
            with nc.Block() as blk:
                def pe(t):
                    t.wait_ge(ldx, 16)
                    for ti in range(NT):
                        pp = pxf[ti % 2]
                        if ti >= 2:
                            t.wait_ge(b_dvex, ti - 1)
                        last = None
                        for dc in range(ND):
                            last = t.transpose(
                                pp[:, dc * 128:(dc + 1) * 128],
                                vbuf[:, ti * DM + dc * 128:
                                     ti * DM + dc * 128 + 128],
                                ident_b[:, :])
                        last.then_inc(b_pex, 1)
                blk.tensor(pe)

                def ve(v):
                    for ti in range(NT):
                        pp = pxf[ti % 2]
                        v.wait_ge(b_pex, ti + 1)
                        d3 = fmbuf[:, :].rearrange("p (c f) -> p c f", c=ND)
                        v.tensor_copy(d3[:, :, ti * 128:ti * 128 + 128],
                                      pp[:, :].rearrange("p (c f) -> p c f",
                                                         c=ND)
                                      ).then_inc(b_dvex, 1)
                blk.vector(ve)

        emit_tap("xfm", fmbuf[:, :])

        def proj_rope(which):
            dst = qall if which == "q" else kall
            base = {"q": 0, "k": 1}[which]
            b0 = base * 8
            with ExitStack() as SS:
                wT = SS.enter_context(
                    nc.sbuf_tensor(f"wT_{which}", [128, ND * DM], BF16))
                with ExitStack() as ST:
                    pwt = [ST.enter_context(nc.psum_tensor(
                        f"pwt{i}_{which}", [128, DM], BF16)) for i in range(2)]
                    with nc.Block() as blk:
                        def pe(t):
                            t.wait_ge(ldw, 16 * (base + 1))
                            for dc in range(ND):
                                pp = pwt[dc % 2]
                                if dc >= 2:
                                    t.wait_ge(b_dve, b0 + dc - 1)
                                last = None
                                for ic in range(ND):
                                    last = t.transpose(
                                        pp[:, ic * 128:(ic + 1) * 128],
                                        wstage[:, ic * DM + dc * 128:
                                               ic * DM + dc * 128 + 128],
                                        ident_b[:, :])
                                last.then_inc(b_pe, 1)
                        blk.tensor(pe)

                        def ve(v):
                            for dc in range(ND):
                                pp = pwt[dc % 2]
                                v.wait_ge(b_pe, b0 + dc + 1)
                                v.tensor_copy(wT[:, dc * DM:(dc + 1) * DM],
                                              pp[:, :]).then_inc(b_dve, 1)
                        blk.vector(ve)
                with ExitStack() as SP:
                    tabA = SP.enter_context(
                        nc.sbuf_tensor(f"tabA_{which}", [128, T], BF16))
                    tabB = SP.enter_context(
                        nc.sbuf_tensor(f"tabB_{which}", [128, T], BF16))
                    qsb = SP.enter_context(
                        nc.sbuf_tensor(f"qsb_{which}", [128, T], BF16))
                    tmpf = SP.enter_context(
                        nc.sbuf_tensor(f"tmpf_{which}", [128, T], F32))
                    pq = SP.enter_context(
                        nc.psum_tensor(f"pq_{which}", [128, T], F32))
                    prot = SP.enter_context(
                        nc.psum_tensor(f"prot_{which}", [128, T], F32))
                    with nc.Block() as blk:
                        def gp(g):
                            srcA = aq_d if which == "q" else ak_d
                            srcB = bq_d if which == "q" else bk_d
                            g.dma_start(tabA[:, :], srcA[:, :]).then_inc(ldtab, 16)
                            g.dma_start(tabB[:, :], srcB[:, :]).then_inc(ldtab, 16)
                        blk.gpsimd(gp)

                        def syw(s):
                            nxt = wkb2_d if which == "q" else wvb2_d
                            s.wait_ge(sp_pcast, 64 if which == "q" else 80)
                            w3s = nxt.rearrange("(c p) f -> p c f", p=128)
                            w3d = wstage[:, :].rearrange("p (c f) -> p c f",
                                                         c=ND)
                            s.dma_start(w3d, w3s).then_inc(ldw, 16)
                        blk.sync(syw)

                        def pe(t):
                            for g in range(ND):
                                gb = b0 + g
                                if g >= 1:
                                    t.wait_ge(b_actQ, gb)
                                last = None
                                for nch in range(4):
                                    for dc in range(ND):
                                        last = t.matmul(
                                            pq[:, nch * 512:(nch + 1) * 512],
                                            wT[:, dc * DM + g * 128:
                                               dc * DM + g * 128 + 128],
                                            fmbuf[:, dc * T + nch * 512:
                                                  dc * T + (nch + 1) * 512],
                                            start=(dc == 0),
                                            stop=(dc == ND - 1))
                                last.then_inc(b_peQ, 1)
                                t.wait_ge(b_actQ, gb + 1)
                                if g >= 1:
                                    t.wait_ge(b_dveQ, gb)
                                last = None
                                for nch in range(4):
                                    last = t.matmul(
                                        prot[:, nch * 512:(nch + 1) * 512],
                                        permT_b[:, :],
                                        qsb[:, nch * 512:(nch + 1) * 512],
                                        start=True, stop=True)
                                last.then_inc(b_peR, 1)
                        blk.tensor(pe)

                        def ac(s):
                            for g in range(ND):
                                gb = b0 + g
                                s.wait_ge(b_peQ, gb + 1)
                                if g >= 1:
                                    s.wait_ge(b_dveQ, gb)
                                s.activation(qsb[:, :], pq[:, :],
                                             AF.Copy).then_inc(b_actQ, 1)
                        blk.scalar(ac)

                        def ve(v):
                            v.wait_ge(ldtab, 32 * (base + 1))
                            for g in range(ND):
                                gb = b0 + g
                                v.wait_ge(b_actQ, gb + 1)
                                v.tensor_mul(tmpf[:, :], qsb[:, :], tabA[:, :])
                                v.wait_ge(b_peR, gb + 1)
                                v.tensor_mul(qsb[:, :], prot[:, :], tabB[:, :])
                                v.tensor_add(dst[:, g * T:(g + 1) * T],
                                             tmpf[:, :],
                                             qsb[:, :]).then_inc(b_dveQ, 1)
                        blk.vector(ve)

        proj_rope("q")
        proj_rope("k")
        emit_tap("qall", qall[:, :])
        emit_tap("kall", kall[:, :])

        with ExitStack() as SV:
            wT = SV.enter_context(nc.sbuf_tensor("wT_v", [128, ND * DM], BF16))
            pwt = [SV.enter_context(nc.psum_tensor(f"pwtv{i}", [128, DM], BF16))
                   for i in range(2)]
            pv = [SV.enter_context(nc.psum_tensor(f"pv{i}", [128, DM], F32))
                  for i in range(2)]
            with nc.Block() as blk:
                def pe(t):
                    t.wait_ge(ldw, 48)
                    for dc in range(ND):
                        pp = pwt[dc % 2]
                        if dc >= 2:
                            t.wait_ge(b_dve, 16 + dc - 1)
                        last = None
                        for ic in range(ND):
                            last = t.transpose(
                                pp[:, ic * 128:(ic + 1) * 128],
                                wstage[:, ic * DM + dc * 128:
                                       ic * DM + dc * 128 + 128],
                                ident_b[:, :])
                        last.then_inc(b_pe, 1)
                    t.wait_ge(b_dve, 24)
                    for ti in range(NT):
                        pp = pv[ti % 2]
                        if ti >= 2:
                            t.wait_ge(v_act, ti - 1)
                        last = None
                        for nch in range(2):
                            for dc in range(ND):
                                last = t.matmul(
                                    pp[:, nch * 512:(nch + 1) * 512],
                                    fmbuf[:, dc * T + ti * 128:
                                          dc * T + ti * 128 + 128],
                                    wT[:, dc * DM + nch * 512:
                                       dc * DM + (nch + 1) * 512],
                                    start=(dc == 0), stop=(dc == ND - 1))
                        last.then_inc(v_pe, 1)
                blk.tensor(pe)

                def ve(v):
                    for dc in range(ND):
                        pp = pwt[dc % 2]
                        v.wait_ge(b_pe, 16 + dc + 1)
                        v.tensor_copy(wT[:, dc * DM:(dc + 1) * DM],
                                      pp[:, :]).then_inc(b_dve, 1)
                blk.vector(ve)

                def ac(s):
                    for ti in range(NT):
                        pp = pv[ti % 2]
                        s.wait_ge(v_pe, ti + 1)
                        s.activation(vbuf[:, ti * DM:(ti + 1) * DM], pp[:, :],
                                     AF.Copy).then_inc(v_act, 1)
                blk.scalar(ac)
        emit_tap("v", vbuf[:, :])
        S1w.close()
        if stage < 2:
            return nc

        with ExitStack() as SC:
            sc_ = lambda name, shape, dt: SC.enter_context(
                nc.sbuf_tensor(name, list(shape), dt))
            pt = [sc_(f"pt{i}", (128, T), BF16) for i in range(4)]
            vpp = sc_("vpp", (128, T), BF16)
            nmx = sc_("nmx", (128, 32), F32)
            nm2 = sc_("nm2", (128, 64), F32)
            sa2 = sc_("sa2", (128, 64), F32)
            sacc = sc_("sacc", (128, 32), F32)
            lnS = sc_("lnS", (128, 32), F32)
            cf = sc_("cf", (128, 32), F32)
            chi = sc_("chi", (128, 32), BF16)
            clo = sc_("clo", (128, 32), BF16)
            thi = sc_("thi", (32, 128), BF16)
            tlo = sc_("tlo", (32, 128), BF16)
            crow2 = sc_("crow2", (2, 2 * T), BF16)
            kaug = sc_("kaug", (128, T), BF16)
            qaug = sc_("qaug", (128, T), BF16)
            lnrow = sc_("lnrow", (1, T), F32)
            utb = sc_("utb", (128, T), F32) if "ut" in taps else None
            rrow = sc_("rrow", (1, T), BF16)
            scrap = pt[0]
            rbc = pt[1]
            psA = SC.enter_context(nc.psum_tensor("psA", [128, T], F32))
            attnT = SC.enter_context(nc.psum_tensor("attnT", [128, T], F32))
            slots = [psA[:, 0:1024], psA[:, 1024:2048],
                     attnT[:, 0:1024], attnT[:, 1024:2048]]
            s1sem = (c_s1a, c_s1b, c_s1c, c_s1d)
            s2sem = (c_s2a, c_s2b)
            pthalf = [pt[i][:, 0:1024] for i in range(4)]

            def jdec(j):
                return (j % 32) // 2, j // 32, j % 2

            with nc.Block() as blk:
                def gp(g):
                    def flat(ap):
                        return ap.rearrange("a b -> (a b)").unsqueeze(0)

                    def cast(m):
                        if m < 48:
                            e, k = divmod(m, 3)
                            src = (gw_d, uw_d, dw_d)[k][e, :, :]
                            dst = (gwb_d, uwb_d, dwb_d)[k][e, :, :]
                            g.dma_start(flat(dst), flat(src)
                                        ).then_inc(sp_wcast, 16)
                        elif m < 51:
                            src, dst = ((shg_d, shgb_d), (shu_d, shub_d),
                                        (shd_d, shdb_d))[m - 48]
                            g.dma_start(flat(dst[:, :]), flat(src[:, :])
                                        ).then_inc(sp_wcast, 32)

                    cast_ord = [48, 49, 50] + list(range(48))
                    nc_per = (51 + ATTN_PAIRS - 1) // ATTN_PAIRS
                    for h in range(ATTN_PAIRS):
                        g.wait_ge(c_ctr, h + 1)
                        g.dma_start(crs_hi[:, :], thi[:, :]).then_inc(c_scr, 16)
                        g.dma_start(crs_lo[:, :], tlo[:, :]).then_inc(c_scr, 16)
                        g.wait_ge(c_scr, 32 * (h + 1))
                        for hd in range(2):
                            fh = crs_hi[hd * 16:(hd + 1) * 16, :].rearrange(
                                "a b -> (a b)").unsqueeze(0)
                            fl = crs_lo[hd * 16:(hd + 1) * 16, :].rearrange(
                                "a b -> (a b)").unsqueeze(0)
                            g.dma_start(crow2[0:1, hd * T:(hd + 1) * T],
                                        fh).then_inc(c_crowD, 16)
                            g.dma_start(crow2[1:2, hd * T:(hd + 1) * T],
                                        fl).then_inc(c_crowD, 16)
                        for m in cast_ord[h * nc_per:(h + 1) * nc_per]:
                            cast(m)
                blk.gpsimd(gp)

                def pe(t):
                    t.wait_ge(a_dve, 1)
                    for h in range(ATTN_PAIRS):
                        qc = h * T
                        for u in range(64):
                            k, sh = u // 2, u % 2
                            ti, hd = k // 2, k % 2
                            r = u % 4
                            if u == 0:
                                t.wait_ge(c_epi, h)
                            t.wait_ge(s1sem[r], h * 16 + u // 4)
                            rows = slice(hd * 64, hd * 64 + 64)
                            last = None
                            for nch in range(2):
                                last = t.matmul(
                                    slots[r][:, nch * 512:(nch + 1) * 512],
                                    qall[rows, qc + ti * 128: qc + ti * 128 + 128],
                                    kall[rows, qc + sh * 1024 + nch * 512:
                                         qc + sh * 1024 + (nch + 1) * 512],
                                    start=True, stop=True)
                            last.then_inc(c_peS, 1)
                        for ss in s1sem:
                            t.wait_ge(ss, h * 16 + 16)
                        t.wait_ge(c_cmat, h + 1)
                        t.transpose(psA[0:32, 0:64].bitcast(BF16), chi[:, :],
                                    ident_b[:, :])
                        t.transpose(psA[0:32, 64:128].bitcast(BF16), clo[:, :],
                                    ident_b[:, :]).then_inc(c_peCT, 1)
                        t.wait_ge(c_crow, h + 1)
                        t.wait_ge(c_crowD, 64 * (h + 1))
                        def pv_round(j):
                            kc, hd, qh = jdec(j)
                            t.wait_ge(s2sem[qh], h * 32 + j // 2 + 1)
                            if hd == 0:
                                lhsT = vbuf[:, kc * DM + h * 128:
                                            kc * DM + h * 128 + 128]
                            else:
                                lhsT = vpp[:, kc * 128:(kc + 1) * 128]
                            last = None
                            for nch in range(2):
                                last = t.matmul(
                                    attnT[:, qh * 1024 + nch * 512:
                                          qh * 1024 + (nch + 1) * 512],
                                    lhsT,
                                    pthalf[j % 4][:, nch * 512:(nch + 1) * 512],
                                    start=(kc == 0 and hd == 0),
                                    stop=(kc == NK - 1 and hd == 1))
                            last.then_inc(c_pv, 1)

                        for j in range(64):
                            kc, hd, qh = jdec(j)
                            if j % 32 == 0:
                                t.wait_ge(c_aug, 2 * h + hd + 1)
                            half = psA[:, qh * 1024:(qh + 1) * 1024]
                            t.wait_ge(s2sem[qh], h * 32 + j // 2)
                            last = None
                            for nch in range(2):
                                last = t.matmul(
                                    half[:, nch * 512:(nch + 1) * 512],
                                    kaug[:, kc * 128:(kc + 1) * 128],
                                    qaug[:, qh * 1024 + nch * 512:
                                         qh * 1024 + (nch + 1) * 512],
                                    start=True, stop=True)
                            last.then_inc(c_peST, 1)
                            if j >= 1:
                                pv_round(j - 1)
                        pv_round(63)
                        t.wait_ge(c_ep1, h + 1)
                        t.wait_ge(c_s2a, h * 32 + 32)
                        t.wait_ge(c_s2b, h * 32 + 32)
                        last = None
                        for nch in range(4):
                            last = t.matmul(psA[0:1, nch * 512:(nch + 1) * 512],
                                            onescol_bf[:, :],
                                            scrap[:, nch * 512:(nch + 1) * 512],
                                            start=True, stop=True)
                        last.then_inc(c_ep2, 1)
                        t.wait_ge(c_ep3, h + 1)
                        last = None
                        for nch in range(4):
                            last = t.matmul(psA[:, nch * 512:(nch + 1) * 512],
                                            onesrow_bf[0:1, :],
                                            rrow[0:1, nch * 512:(nch + 1) * 512],
                                            start=True, stop=True)
                        last.then_inc(c_ep4, 1)
                blk.tensor(pe)

                def ve(v):
                    v.memset(kaug[64:128, :], 0.0)
                    v.memset(kaug[64:66, :], 1.0)
                    v.memset(qaug[64:128, :], 0.0)

                    def augbuild(h, hd):
                        v.wait_ge(c_peST, h * 64 + hd * 32)
                        v.wait_ge(c_crowD, 64 * h + 32 * (hd + 1))
                        rows = slice(hd * 64, hd * 64 + 64)
                        v.tensor_copy(kaug[0:64, :], kall[rows, h * T:(h + 1) * T])
                        v.tensor_copy(qaug[0:64, :], qall[rows, h * T:(h + 1) * T])
                        v.tensor_copy(qaug[64:66, :],
                                      crow2[0:2, hd * T:(hd + 1) * T]
                                      ).then_inc(c_aug, 1)

                    for h in range(ATTN_PAIRS):
                        for u in range(64):
                            k, sh = u // 2, u % 2
                            ti, hd = k // 2, k % 2
                            r = u % 4
                            cc = hd * 16 + ti
                            v.wait_ge(c_peS, h * 64 + u + 1)
                            inst = v.reduce_max(nm2[:, 2 * cc + sh:2 * cc + sh + 1],
                                                slots[r][:, :], axis=AX.X,
                                                negate=True)
                            if sh == 0:
                                inst.then_inc(c_max, 1)
                            else:
                                v.drain()
                                v.tensor_tensor(nmx[:, cc:cc + 1],
                                                nm2[:, 2 * cc:2 * cc + 1],
                                                nm2[:, 2 * cc + 1:2 * cc + 2],
                                                OP.min).then_inc(c_max, 1)
                        for ss in s1sem:
                            v.wait_ge(ss, h * 16 + 16)
                        v.tensor_add(sacc[:, :], sa2[:, 0:64:2],
                                     sa2[:, 1:64:2]).then_inc(c_sumc, 1)
                        v.wait_ge(c_lnS, h + 1)
                        if h >= 1:
                            v.wait_ge(c_crowD, 64 * h)
                        v.tensor_sub(cf[:, :], nmx[:, :], lnS[:, :])
                        v.drain()
                        v.tensor_copy(chi[:, :], cf[:, :])
                        v.drain()
                        v.tensor_sub(clo[:, :], cf[:, :],
                                     chi[:, :]).then_inc(c_cmat, 1)
                        v.wait_ge(c_peCT, h + 1)
                        v.tensor_copy(thi[:, :], psA[0:32, 0:64].bitcast(BF16))
                        v.tensor_copy(tlo[:, :], psA[0:32, 64:128].bitcast(BF16)
                                      ).then_inc(c_ctr, 1)
                        v3o = vpp[:, :].rearrange("p (c f) -> p c f", c=NK)
                        v3i = vbuf[:, :].rearrange("p (c f) -> p c f", c=NK)
                        v.tensor_scalar_mul(v3o,
                                            v3i[:, :, h * 128:h * 128 + 128],
                                            neglam[:, :]).then_inc(c_crow, 1)
                        augbuild(h, 0)
                        augbuild(h, 1)
                        v.wait_ge(c_ep4, h + 1)
                        if utb is not None and h == ATTN_PAIRS - 1:
                            v.tensor_copy(utb[:, :], attnT[:, :])
                        v.tensor_copy(rbc[:, :], psA[:, :])
                        v.drain()
                        v.tensor_mul(fmbuf[:, h * T:(h + 1) * T], attnT[:, :],
                                     rbc[:, :]).then_inc(c_epi, 1)
                blk.vector(ve)

                def ac(s):
                    for h in range(ATTN_PAIRS):
                        for u in range(64):
                            k, sh = u // 2, u % 2
                            ti, hd = k // 2, k % 2
                            r = u % 4
                            cc = hd * 16 + ti
                            s.wait_ge(c_max, h * 64 + 2 * k + 2)
                            s.activation(scrap[:, 0:1024], slots[r][:, :],
                                         AF.Exp, bias=nmx[:, cc:cc + 1],
                                         accum_out=sa2[:, 2 * cc + sh:
                                                       2 * cc + sh + 1]
                                         ).then_inc(s1sem[r], 1)
                        s.wait_ge(c_sumc, h + 1)
                        s.activation(lnS[:, :], sacc[:, :],
                                     AF.Ln).then_inc(c_lnS, 1)
                        for j in range(64):
                            kc, hd, qh = jdec(j)
                            s.wait_ge(c_peST, h * 64 + j + 1)
                            s.wait_ge(c_pv, max(0, h * 64 + j - 3))
                            s.wait_ge(c_epi, h)
                            s.activation(pthalf[j % 4][:, :],
                                         psA[:, qh * 1024:(qh + 1) * 1024],
                                         AF.Exp).then_inc(s2sem[qh], 1)
                        s.wait_ge(c_pv, h * 64 + 64)
                        s.activation(scrap[:, :], attnT[:, :],
                                     AF.Square).then_inc(c_ep1, 1)
                        s.wait_ge(c_ep2, h + 1)
                        s.activation(lnrow[0:1, :], psA[0:1, :], AF.Ln,
                                     bias=constb[0:1, 0:1],
                                     scale=1.0 / 128.0)
                        s.drain()
                        s.activation(rrow[0:1, :], lnrow[0:1, :], AF.Exp,
                                     bias=constb[0:1, 1:2],
                                     scale=-0.5).then_inc(c_ep3, 1)
                blk.scalar(ac)

            emit_tap("attnfm", fmbuf[:, :])
            emit_tap("sa", sacc[:, :])
            emit_tap("crow", crow2[0:1, :])
            emit_tap("crowlo", crow2[1:2, :])
            for _nm, _ap in (("nmx", nmx), ("lnS", lnS), ("chi", chi),
                             ("clo", clo), ("thi", thi), ("tlo", tlo),
                             ("cf", cf)):
                emit_tap(_nm, _ap[:, :])
            if utb is not None:
                emit_tap("ut", utb[:, :])
            emit_tap("ptb1", pt[3][:, :])
        S1.close()
        if stage < 3:
            return nc

        S2 = S.enter_context(ExitStack())
        acc = S2.enter_context(nc.sbuf_tensor("acc", [128, NT * DM], F32))
        postm_i = S2.enter_context(nc.sbuf_tensor("postm_i", [128, 256],
                                                  mybir.dt.int32))
        valstore = S2.enter_context(nc.sbuf_tensor("valstore", [128, 256],
                                                   F32))
        tokid_f = S2.enter_context(nc.sbuf_tensor("tokid_f", [128, 16], F32))
        tokid_i = S2.enter_context(nc.sbuf_tensor("tokid_i", [128, 16],
                                                  mybir.dt.int32))
        S2R = S.enter_context(ExitStack())
        x1fm = S2R.enter_context(
            nc.sbuf_tensor("x1fm", [128, ND * T], BF16, side="right"))

        def wsrc(e, k):
            if k < 2:
                if e < 16:
                    ap = (gwb_d if k == 0 else uwb_d)[e, :, :]
                else:
                    sh = shgb_d if k == 0 else shub_d
                    ap = sh[:, (e - 16) * DM:(e - 15) * DM]
            else:
                if e < 16:
                    ap = dwb_d[e, :, :]
                else:
                    ap = shdb_d[(e - 16) * DM:(e - 15) * DM, :]
            return ap.rearrange("(c p) f -> p c f", p=128)
        with ExitStack() as SD:
            sd_ = lambda name, shape, dt: SD.enter_context(
                nc.sbuf_tensor(name, list(shape), dt))
            scores_sb = sd_("scores_sb", (128, 256), F32)
            xst = [sd_(f"xst{i}", (128, DM), F32) for i in range(2)]
            x1b = [sd_(f"x1b{i}", (128, DM), BF16) for i in range(2)]
            pkw = sd_("pkw", (16, 4 * T), F32)
            selstore = sd_("selstore", (128, 256), F32)
            z16 = sd_("z16", (128, 16), F32)
            postm = sd_("postm", (128, 256), F32)
            zstage = sd_("zstage", (128, DM), F32)
            p2048 = sd_("p2048", (128, 96), F32)
            identf = sd_("identf", (128, 128), F32)
            eoffb = sd_("eoffb", (128, 1), F32)
            bn12 = sd_("bn12", (128, 12), F32)
            mv = sd_("mv", (128, 2), F32)
            rsq = sd_("rsq", (128, 1), F32)
            sb16 = sd_("sb16", (128, 16), F32)
            t16a = sd_("t16a", (128, 16), F32)
            t16b = sd_("t16b", (128, 16), F32)
            selb = sd_("selb", (128, 16), F32)
            emsk = sd_("emsk", (128, 16), F32)
            gs4 = sd_("gs4", (128, 4), F32)
            g4 = sd_("g4", (128, 4), F32)
            t4 = sd_("t4", (128, 4), F32)
            w2 = sd_("w2", (128, 2), F32)
            psO = [SD.enter_context(nc.psum_tensor(f"psO{i}", [128, DM], F32))
                   for i in range(2)]
            psT = [SD.enter_context(nc.psum_tensor(f"psT{i}", [128, DM], BF16))
                   for i in range(2)]
            psR = [SD.enter_context(nc.psum_tensor(f"psR{i}", [128, 128], F32))
                   for i in range(2)]
            with nc.Block() as blk:
                def sy(s):
                    def x1st(tj):
                        s.wait_ge(d_x1b, tj + 1)
                        s.dma_start(x1t_d[tj * 128:(tj + 1) * 128, :],
                                    x1b[tj % 2][:, :]
                                    ).then_inc(sp_x1d[tj % 2], 16)
                    for ti in range(NT):
                        if ti >= 2:
                            s.wait_ge(d_add, ti - 1)
                        s.dma_start(xst[ti % 2][:, :],
                                    x_d[ti * 128:(ti + 1) * 128, :]
                                    ).then_inc(d_xld[ti % 2], 16)
                        if ti >= 2:
                            x1st(ti - 2)
                    x1st(NT - 2)
                    x1st(NT - 1)
                blk.sync(sy)

                def gp(g):
                    g.dma_start(tokid_f[:, :], tokid_d[:, :]
                                ).then_inc(sp_dmisc, 16)
                    g.dma_start(identf[:, :], ident_d[:, :]
                                ).then_inc(sp_dmisc, 16)
                    g.dma_start(eoffb[:, :], eoff_d[:, :]
                                ).then_inc(sp_dmisc, 16)
                    g.wait_ge(sp_z, 3)
                    for k in range(4):
                        for ti in range(NT):
                            r0 = k * 4096 + ti * 128
                            if r0 + 128 <= 3 * 4096 + T:
                                g.dma_start(yrank_d[r0:r0 + 128, :],
                                            zstage[:, :]).then_inc(sp_zacc, 16)
                    g.dma_start(slotl_d[0:DUMP, :].rearrange(
                        "(p c) f -> p (c f)", c=96),
                        p2048[:, :]).then_inc(sp_zacc, 16)
                    g.dma_start(x1t_d[T:T + 1, :],
                                zstage[0:1, :]).then_inc(sp_zacc, 16)
                    g.dma_start(cmbd_d[T:T + 1, :],
                                zstage[0:1, 0:16]).then_inc(sp_cmb, 16)
                    for ti in range(NT):
                        g.wait_ge(r_dve, ti + 1)
                        g.dma_start(cmbd_d[ti * 128:(ti + 1) * 128, :],
                                    cmb[:, ti * 16:(ti + 1) * 16]
                                    ).then_inc(sp_cmb, 16)
                blk.gpsimd(gp)

                def pe(t):
                    def otile(ti):
                        if ti >= 2:
                            t.wait_ge(d_add, ti - 1)
                        last = None
                        for nch in range(2):
                            for jc in range(ND):
                                last = t.matmul(
                                    psO[ti % 2][:, nch * 512:(nch + 1) * 512],
                                    fmbuf[:, jc * T + ti * 128:
                                          jc * T + ti * 128 + 128],
                                    woT[:, jc * DM + nch * 512:
                                        jc * DM + (nch + 1) * 512],
                                    start=(jc == 0), stop=(jc == ND - 1))
                        last.then_inc(d_peO, 1)

                    def ttile(ti):
                        t.wait_ge(d_x1b, ti + 1)
                        if ti >= 2:
                            t.wait_ge(d_evT, ti - 1)
                        last = None
                        for dc in range(ND):
                            last = t.transpose(
                                psT[ti % 2][:, dc * 128:(dc + 1) * 128],
                                x1b[ti % 2][:, dc * 128:(dc + 1) * 128],
                                ident_b[:, :])
                        last.then_inc(d_peT, 1)

                    def rtile(ti):
                        t.wait_ge(d_evT, ti + 1)
                        if ti >= 2:
                            t.wait_ge(r_act, ti - 1)
                        last = None
                        for dc in range(ND):
                            last = t.matmul(
                                psR[ti % 2][:, 0:16],
                                x1fm[:, dc * T + ti * 128:
                                     dc * T + ti * 128 + 128],
                                routerT[:, dc * 16:(dc + 1) * 16],
                                start=(dc == 0), stop=(dc == ND - 1))
                        last.then_inc(r_pe, 1)

                    for ti in range(NT):
                        otile(ti)
                        if ti >= 2:
                            ttile(ti - 2)
                        if ti >= 4:
                            rtile(ti - 4)
                    ttile(NT - 2)
                    ttile(NT - 1)
                    for ti in range(NT - 4, NT):
                        rtile(ti)
                    t.wait_ge(r_act, NT)
                    t.wait_ge(sp_dmisc, 48)
                    for ti in range(NT):
                        t.wait_ge(r_dve, ti + 1)
                        if ti >= 2:
                            t.wait_ge(d_selE, ti - 1)
                        t.transpose(psR[ti % 2][0:16, 0:128],
                                    selstore[:, ti * 16:(ti + 1) * 16],
                                    identf[:, :]).then_inc(d_selT, 1)
                    t.wait_ge(d_pos, 1)
                    for tu in range(NT):
                        if tu >= 2:
                            t.wait_ge(d_posE, tu - 1)
                        t.transpose(psR[tu % 2][:, 0:16],
                                    pkw[0:16, 2 * T + tu * 128:
                                        2 * T + (tu + 1) * 128],
                                    identf[0:16, 0:16]).then_inc(d_posT, 1)
                blk.tensor(pe)

                def ve(v):
                    v.wait_ge(sp_dmisc, 48)
                    v.memset(pkw[0:16, 3 * T:4 * T], 0.0).then_inc(sp_z, 1)
                    v.memset(zstage[:, :], 0.0).then_inc(sp_z, 1)
                    v.memset(p2048[:, :], float(3 * 4096 + T)).then_inc(sp_z, 1)
                    v.memset(z16[:, :], 0.0)

                    def topk(ti):
                        v.wait_ge(r_act, ti + 1)
                        sc = scores_sb[:, ti * 16:(ti + 1) * 16]
                        s3 = sb16[:, :].rearrange("p (g c) -> p g c", g=4)
                        a3 = t16a[:, :].rearrange("p (g c) -> p g c", g=4)
                        b3 = t16b[:, :].rearrange("p (g c) -> p g c", g=4)
                        e3 = emsk[:, :].rearrange("p (g c) -> p g c", g=4)
                        gm1b = gs4[:, :].unsqueeze(2).broadcast_to([128, 4, 4])
                        g4b = g4[:, :].unsqueeze(2).broadcast_to([128, 4, 4])
                        v.tensor_add(sb16[:, :], sc, biasbc[:, :])
                        v.drain()
                        v.tensor_reduce(gs4[:, :], s3, axis=AX.X, op=OP.max)
                        v.drain()
                        v.tensor_tensor(a3, s3, gm1b, OP.is_equal)
                        v.drain()
                        v.scalar_tensor_tensor(t16b[:, :], t16a[:, :], -1e4,
                                               sb16[:, :], OP.mult, OP.add)
                        v.drain()
                        v.tensor_reduce(t4[:, :], b3, axis=AX.X, op=OP.max)
                        v.drain()
                        v.tensor_add(gs4[:, :], gs4[:, :], t4[:, :])
                        v.drain()
                        v.reduce_max(w2[:, 0:1], gs4[:, :], axis=AX.X)
                        v.drain()
                        v.tensor_scalar(g4[:, :], gs4[:, :], w2[:, 0:1], None,
                                        OP.is_equal)
                        v.drain()
                        v.scalar_tensor_tensor(g4[:, :], g4[:, :], -1e4,
                                               gs4[:, :], OP.mult, OP.add)
                        v.drain()
                        v.reduce_max(w2[:, 1:2], g4[:, :], axis=AX.X)
                        v.drain()
                        v.tensor_scalar(g4[:, :], gs4[:, :], w2[:, 1:2], None,
                                        OP.is_ge)
                        v.drain()
                        v.tensor_copy(e3, g4b)
                        v.tensor_scalar_add(t16a[:, :], sb16[:, :], 1.0)
                        v.drain()
                        v.tensor_mul(t16a[:, :], t16a[:, :], emsk[:, :])
                        v.drain()
                        v.tensor_scalar_add(t16a[:, :], t16a[:, :], -1.0)
                        v.memset(selb[:, :], 0.0)
                        v.drain()
                        for _ in range(4):
                            v.reduce_max(w2[:, 0:1], t16a[:, :], axis=AX.X)
                            v.drain()
                            v.tensor_scalar(t16b[:, :], t16a[:, :],
                                            w2[:, 0:1], None, OP.is_equal)
                            v.drain()
                            v.tensor_add(selb[:, :], selb[:, :], t16b[:, :])
                            v.scalar_tensor_tensor(t16a[:, :], t16b[:, :],
                                                   -1e5, t16a[:, :],
                                                   OP.mult, OP.add)
                            v.drain()
                        sst = selstore[:, ti * 16:(ti + 1) * 16]
                        v.tensor_scalar_min(sst, selb[:, :], 1.0)
                        v.drain()
                        v.tensor_tensor_scan(t16a[:, :], sst, z16[:, :],
                                             0.0, OP.add, OP.add)
                        v.drain()
                        v.tensor_sub(t16a[:, :], t16a[:, :], sst)
                        v.drain()
                        v.tensor_scalar(valstore[:, ti * 16:(ti + 1) * 16],
                                        t16a[:, :], 4096.0,
                                        tokid_f[:, ti:ti + 1],
                                        OP.mult, OP.add)
                        v.tensor_mul(t16b[:, :], sc, sst)
                        v.drain()
                        v.reduce_sum(w2[:, 0:1], t16b[:, :], axis=AX.X)
                        v.drain()
                        v.tensor_scalar_add(w2[:, 0:1], w2[:, 0:1], 1e-20)
                        v.drain()
                        v.reciprocal(w2[:, 1:2], w2[:, 0:1])
                        v.drain()
                        v.tensor_scalar(cmb[:, ti * 16:(ti + 1) * 16],
                                        t16b[:, :], w2[:, 1:2], 2.5,
                                        OP.mult, OP.mult).then_inc(r_dve, 1)

                    for ti in range(NT):
                        sl = slice(ti * DM, (ti + 1) * DM)
                        v.wait_ge(d_peO, ti + 1)
                        v.wait_ge(d_xld[ti % 2], 16 * (ti // 2 + 1))
                        v.tensor_add(acc[:, sl], psO[ti % 2][:, :],
                                     xst[ti % 2][:, :]).then_inc(d_add, 1)
                        v.bn_stats(bn12[:, 0:6], acc[:, ti * DM:ti * DM + 512])
                        v.bn_stats(bn12[:, 6:12],
                                   acc[:, ti * DM + 512:(ti + 1) * DM])
                        v.drain()
                        v.bn_aggr(mv[:, :], bn12[:, :]).then_inc(d_bn, 1)
                        v.wait_ge(d_act, ti + 1)
                        v.tensor_scalar(acc[:, sl], acc[:, sl], mv[:, 0:1],
                                        rsq[:, 0:1], OP.subtract, OP.mult)
                        if ti >= 2:
                            v.wait_ge(d_peT, ti - 1)
                        if ti >= 2:
                            v.wait_ge(sp_x1d[ti % 2], 16 * (ti // 2))
                        v.tensor_copy(x1b[ti % 2][:, :],
                                      acc[:, sl]).then_inc(d_x1b, 1)
                        if ti >= 2:
                            v.wait_ge(d_peT, ti - 1)
                            d3 = x1fm[:, :].rearrange("p (c f) -> p c f", c=ND)
                            v.tensor_copy(d3[:, :, (ti - 2) * 128:(ti - 2) * 128 + 128],
                                          psT[ti % 2][:, :].rearrange(
                                              "p (c f) -> p c f", c=ND)
                                          ).then_inc(d_evT, 1)
                        if ti >= 6:
                            topk(ti - 6)
                    for ti in (NT - 2, NT - 1):
                        v.wait_ge(d_peT, ti + 1)
                        d3 = x1fm[:, :].rearrange("p (c f) -> p c f", c=ND)
                        v.tensor_copy(d3[:, :, ti * 128:ti * 128 + 128],
                                      psT[ti % 2][:, :].rearrange(
                                          "p (c f) -> p c f", c=ND)
                                      ).then_inc(d_evT, 1)
                    for ti in range(NT - 6, NT):
                        topk(ti)
                    v.wait_ge(sp_dmisc, 48)
                    bsel = pkw[0:16, 0:T]
                    bcum = pkw[0:16, T:2 * T]
                    bpos = pkw[0:16, 2 * T:3 * T]
                    bzz = pkw[0:16, 3 * T:4 * T]
                    for ti in range(NT):
                        v.wait_ge(d_selT, ti + 1)
                        v.tensor_copy(pkw[0:16, ti * 128:(ti + 1) * 128],
                                      psR[ti % 2][0:16, 0:128]
                                      ).then_inc(d_selE, 1)
                    v.drain()
                    v.tensor_tensor_scan(bcum, bsel, bzz, 0.0,
                                         OP.add, OP.add)
                    v.drain()
                    v.tensor_scalar(bpos, bcum, -1.0, float(CAP - 1),
                                    OP.add, OP.min)
                    v.drain()
                    v.tensor_scalar(bcum, bpos, -63.5, 1.0 / 128.0,
                                    OP.add, OP.mult)
                    v.drain()
                    v.tensor_copy(pkw[0:16, T:2 * T].bitcast(mybir.dt.int32),
                                  bcum)
                    v.drain()
                    v.tensor_copy(bcum,
                                  pkw[0:16, T:2 * T].bitcast(mybir.dt.int32))
                    v.drain()
                    v.scalar_tensor_tensor(bpos, bcum, -128.0, bpos,
                                           OP.mult, OP.add)
                    v.drain()
                    v.scalar_tensor_tensor(bpos, bpos, 96.0, bcum,
                                           OP.mult, OP.add)
                    v.drain()
                    v.tensor_scalar(bpos, bpos, eoffb[0:16, 0:1],
                                    float(-DUMP), OP.add, OP.add)
                    v.drain()
                    v.tensor_mul(bpos, bpos, bsel)
                    v.drain()
                    v.tensor_scalar_add(bpos, bpos,
                                        float(DUMP)).then_inc(d_pos, 1)
                    for tu in range(NT):
                        v.wait_ge(d_posT, tu + 1)
                        v.tensor_copy(postm[:, tu * 16:(tu + 1) * 16],
                                      psR[tu % 2][:, 0:16]).then_inc(d_posE, 1)
                    v.drain()
                    v.tensor_copy(postm_i[:, :], postm[:, :])
                    v.tensor_copy(tokid_i[:, :], tokid_f[:, :]
                                  ).then_inc(d_pos, 1)
                blk.vector(ve)

                def ac(s):
                    def sig(ti):
                        s.wait_ge(r_pe, ti + 1)
                        s.activation(scores_sb[:, ti * 16:(ti + 1) * 16],
                                     psR[ti % 2][:, 0:16],
                                     AF.Sigmoid).then_inc(r_act, 1)

                    for ti in range(NT):
                        s.wait_ge(d_bn, ti + 1)
                        if ti >= 1:
                            s.wait_ge(d_x1b, ti)
                        s.activation(rsq[:, :], mv[:, 1:2], AF.Ln,
                                     bias=constb[:, 0:1])
                        s.drain()
                        s.activation(rsq[:, :], rsq[:, :], AF.Exp,
                                     scale=-0.5).then_inc(d_act, 1)
                        if ti >= 5:
                            sig(ti - 5)
                    for ti in range(NT - 5, NT):
                        sig(ti)
                blk.scalar(ac)
            emit_tap("postm", postm[:, :])
            emit_tap("selst", selstore[:, :])
        emit_tap("acc", acc[:, :])
        emit_tap("x1fm", x1fm[:, :])
        emit_tap("cmb", cmb[:, :])
        S2R.close()
        SR_.close()
        if stage < 4:
            return nc

        pes = []
        y0 = ry0 = sc0 = sl0 = xg0 = wg0 = tr0 = ro0 = 0
        for i in range(24):
            routed = i >= 8
            e = (i - 8) if routed else 16 + i // 4
            q = 0 if routed else i % 4
            nt = 6 if routed else 4
            rec = dict(i=i, routed=routed, e=e, q=q, ro=ro0,
                       cap=CAP if routed else 512, nt=nt,
                       mg=3 * e, mu=3 * e + 1, md=3 * e + 2,
                       ybase=y0, rybase=ry0, scat0=sc0, slotl0=sl0,
                       xg0=xg0, wg0=wg0, tr0=tr0)
            lastq = (not routed) and q == 3
            rec["wret_g"] = 2 if (routed or lastq) else 0
            rec["wret_y"] = 1 if (routed or lastq) else 0
            pes.append(rec)
            y0 += nt
            ry0 += nt if routed else 0
            sc0 += 16 if routed else 0
            sl0 += 1 if routed else 0
            xg0 += nt
            wg0 += nt if routed else 0
            tr0 += nt
            ro0 += 1 if routed else 0
        RYT = ry0
        LORD = [48, 49, 50, 51, 52, 53] + list(range(48))
        POS = {m: p for p, m in enumerate(LORD)}
        WTHR = {}
        for m in range(54):
            if m >= 48:
                WTHR[m] = 32 * (min(m, 50) - 47)
            else:
                WTHR[m] = 96 + 16 * (m + 1)
        for m in (51, 52, 53):
            WTHR[m] = 32 * (m - 50)

        with ExitStack() as SE:
            se_ = lambda name, shape, dt: SE.enter_context(
                nc.sbuf_tensor(name, list(shape), dt))
            ge_buf = se_("ge_buf", (128, ND * CAP), BF16)
            hs = [se_(f"hs{i}", (128, CAP), F32) for i in range(2)]
            wsl = [se_(f"wsl{si}", (128, ND * DM), BF16) for si in range(4)]
            x1e_fm = [se_(f"x1efm{i}", (128, ND * CAP), BF16)
                      for i in range(2)]
            x1e_tm = se_("x1etm", (128, 6 * DM), BF16)
            ystage = [se_(f"ystage{i}", (128, DM), F32) for i in range(3)]
            slotl_sb = [se_(f"slotl{i}", (128, 6), F32) for i in range(4)]
            slotli = [se_(f"slotli{i}", (128, 6), mybir.dt.int32)
                      for i in range(4)]
            widx = [se_(f"widx{i}", (128, 6), mybir.dt.int32)
                    for i in range(4)]
            ydst = [se_(f"ydst{i}", (128, 6), mybir.dt.int32)
                    for i in range(4)]
            tmpf = se_("tmpf", (128, 6), F32)
            tmpk = se_("tmpk", (128, 6), F32)
            wg_sb = [se_(f"wg{i}", (128, 6), F32) for i in range(4)]
            psH = SE.enter_context(nc.psum_tensor("psH", [128, DM], F32))
            psU = SE.enter_context(nc.psum_tensor("psU", [128, DM], F32))
            psY = [SE.enter_context(nc.psum_tensor(f"psY{i}", [128, DM], F32))
                   for i in range(2)]

            def wslot(m):
                return wsl[POS[m] % 4], POS[m] // 4 + 1

            with nc.Block() as blk:
                def gp(g):
                    bregs = []

                    firstA = [True]

                    def prepA(j):
                        if not (0 <= j < 24 and pes[j]["routed"]):
                            return
                        if firstA[0]:
                            firstA[0] = False
                            g.wait_ge(sp_zacc, 16 * 66)
                            g.wait_ge(sp_dmisc, 48)
                        e_ = pes[j]["e"]
                        for tau in range(16):
                            col = tau * 16 + e_
                            g.indirect_dma_start(
                                slotl_d[:, 0:1],
                                IndirectOffsetOnAxis(
                                    postm_i[:, col:col + 1], 0),
                                valstore[:, col:col + 1], None,
                                bounds_check=bregs[0], oob_is_err=False
                                ).then_inc(sp_scat, 16)

                    def slotload(j):
                        if not (0 <= j < 24 and pes[j]["routed"]):
                            return
                        e_ = pes[j]["e"]
                        g.wait_ge(sp_scat, 16 * (pes[j]["scat0"] + 16))
                        g.dma_start(
                            slotl_sb[j % 4][:, :],
                            slotl_d[0:DUMP, :].rearrange(
                                "(p c) f -> p (c f)", c=96)
                            [:, e_ * 6:(e_ + 1) * 6]
                            ).then_inc(sp_slotl, 16)

                    def prepB(j):
                        rec = pes[j]
                        if j == 0:
                            g.wait_ge(sp_cmb, 16 * 17)
                            g.wait_ge(sp_x1d[0], 16 * 8)
                            g.wait_ge(sp_x1d[1], 16 * 8)
                        if rec["routed"]:
                            e_ = rec["e"]
                            g.wait_ge(d_slotc, rec["ro"] + 1)
                            g.wait_ge(sp_xts, 16 * rec["tr0"])
                            for r in range(6):
                                g.indirect_dma_start(
                                    x1e_tm[:, r * DM:(r + 1) * DM], None,
                                    x1t_d[:, :],
                                    IndirectOffsetOnAxis(
                                        slotli[j % 4][:, r:r + 1], 0)
                                    ).then_inc(sp_xg, 16)
                            cmbflat = cmbd_d[:, :].rearrange(
                                "a b -> (a b)").unsqueeze(1)
                            for r in range(6):
                                g.indirect_dma_start(
                                    wg_sb[j % 4][:, r:r + 1], None,
                                    cmbflat,
                                    IndirectOffsetOnAxis(
                                        widx[j % 4][:, r:r + 1], 0)
                                    ).then_inc(sp_wg, 16)
                        else:
                            g.wait_ge(sp_xts, 16 * rec["tr0"])
                            for r in range(4):
                                row0 = (rec["q"] * 4 + r) * 128
                                g.dma_start(x1e_tm[:, r * DM:(r + 1) * DM],
                                            x1t_d[row0:row0 + 128, :]
                                            ).then_inc(sp_xg, 16)

                    def sadds(i):
                        rec = pes[i]
                        if not rec["routed"]:
                            return
                        for r in range(rec["nt"]):
                            ridx = rec["rybase"] + r
                            g.wait_ge(e_yev, rec["ybase"] + r + 1)
                            g.indirect_dma_start(
                                yrank_d[:, :],
                                IndirectOffsetOnAxis(
                                    ydst[i % 4][:, r:r + 1], 0),
                                ystage[ridx % 3][:, :], None,
                                bounds_check=bregs[1], oob_is_err=False
                                ).then_inc(sp_sadd, 16)

                    with g.register() as b0, g.register() as b1:
                        bregs.append(b0)
                        bregs.append(b1)
                        g.reg_mov(b0, DUMP - 1)
                        g.reg_mov(b1, 3 * 4096 + T - 1)
                        prepB(0)
                        prepB(1)
                        for i in range(24):
                            prepA(i + 4)
                            slotload(i + 3)
                            if i + 2 < 24:
                                prepB(i + 2)
                            sadds(i)
                blk.gpsimd(gp)

                def pe(t):
                    for rec in pes:
                        i = rec["i"]
                        cap = rec["cap"]
                        slg, geng = wslot(rec["mg"])
                        slu, genu = wslot(rec["mu"])
                        sld, gend = wslot(rec["md"])
                        t.wait_ge(e_wld[POS[rec["mg"]] % 4], 16 * geng)
                        t.wait_ge(e_wld[POS[rec["mu"]] % 4], 16 * genu)
                        t.wait_ge(sp_xts, 16 * (rec["tr0"] + rec["nt"]))
                        fmb = x1e_fm[i % 2]
                        nchl = ([(0, 512), (512, 256)] if cap == CAP
                                else [(0, 512)])
                        for fc in range(ND):
                            n = i * ND + fc
                            for k, ps, semw, sl_ in ((0, psH, e_silu, slg),
                                                     (1, psU, e_gmul, slu)):
                                t.wait_ge(semw, n)
                                last = None
                                for off, w in nchl:
                                    for dc in range(ND):
                                        last = t.matmul(
                                            ps[:, off:off + w],
                                            sl_[:, dc * DM + fc * 128:
                                                dc * DM + fc * 128 + 128],
                                            fmb[:, dc * CAP + off:
                                                dc * CAP + off + w],
                                            start=(dc == 0),
                                            stop=(dc == ND - 1))
                                last.then_inc(e_peH if k == 0 else e_peU, 1)
                        t.nop().then_inc(sp_gdone, 1)
                        if rec["wret_g"]:
                            t.nop().then_inc(e_wdone, rec["wret_g"])
                        t.wait_ge(e_gmul, (i + 1) * ND)
                        t.wait_ge(e_wld[POS[rec["md"]] % 4], 16 * gend)
                        for r in range(rec["nt"]):
                            yidx = rec["ybase"] + r
                            t.wait_ge(e_yev, max(0, yidx - 1))
                            last = None
                            for nch in range(2):
                                for fc in range(ND):
                                    last = t.matmul(
                                        psY[yidx % 2][:, nch * 512:
                                                      (nch + 1) * 512],
                                        ge_buf[:, fc * CAP + r * 128:
                                               fc * CAP + r * 128 + 128],
                                        sld[:, fc * DM + nch * 512:
                                            fc * DM + (nch + 1) * 512],
                                        start=(fc == 0), stop=(fc == ND - 1))
                            last.then_inc(e_peY, 1)
                        if rec["wret_y"]:
                            t.nop().then_inc(e_wdone, rec["wret_y"])
                blk.tensor(pe)

                def ac(s):
                    emitted = set()

                    def load(m):
                        if not (0 <= m < 54) or m in emitted:
                            return
                        emitted.add(m)
                        e_, k = divmod(m, 3)
                        s.wait_ge(e_wdone, max(0, POS[m] - 3))
                        s.wait_ge(sp_wcast, WTHR[m])
                        sl_, _ = wslot(m)
                        d3 = sl_[:, :].rearrange("p (c f) -> p c f", c=ND)
                        s.dma_start(d3, wsrc(e_, k)
                                    ).then_inc(e_wld[POS[m] % 4], 16)

                    for m in LORD[0:4]:
                        load(m)
                    for rec in pes:
                        cap = rec["cap"]
                        j = rec["i"]
                        for fc in range(ND):
                            n = j * ND + fc
                            s.wait_ge(e_peH, n + 1)
                            s.wait_ge(e_gmul, max(0, n - 1))
                            s.activation(hs[n % 2][:, 0:cap], psH[:, 0:cap],
                                         AF.Silu).then_inc(e_silu, 1)
                        if j + 1 < 24:
                            load(3 * pes[j + 1]["e"] + 1)
                            load(3 * pes[j + 1]["e"] + 2)
                        if j + 2 < 24:
                            load(3 * pes[j + 2]["e"])
                    assert len(emitted) == 54, len(emitted)
                blk.scalar(ac)

                def ve(v):
                    def slotcast(j):
                        if not (0 <= j < 24 and pes[j]["routed"]):
                            return
                        if pes[j]["ro"] >= 4:
                            v.wait_ge(sp_sadd, 16 * (pes[j - 4]["rybase"]
                                                     + pes[j - 4]["nt"]))
                        v.wait_ge(sp_slotl, 16 * (pes[j]["slotl0"] + 1))
                        vv = slotl_sb[j % 4][:, :]
                        v.tensor_copy(ydst[j % 4][:, :], vv)
                        v.tensor_scalar(tmpk[:, :], vv, -2047.5,
                                        1.0 / 4096.0, OP.add, OP.mult)
                        v.drain()
                        v.tensor_copy(tmpk[:, :].bitcast(mybir.dt.int32),
                                      tmpk[:, :])
                        v.drain()
                        v.tensor_copy(tmpk[:, :],
                                      tmpk[:, :].bitcast(mybir.dt.int32))
                        v.drain()
                        v.scalar_tensor_tensor(tmpf[:, :], tmpk[:, :],
                                               -4096.0, vv,
                                               OP.mult, OP.add)
                        v.drain()
                        v.tensor_copy(slotli[j % 4][:, :], tmpf[:, :])
                        v.tensor_scalar(tmpk[:, :], tmpf[:, :],
                                        16.0, float(pes[j]["e"]),
                                        OP.mult, OP.add)
                        v.drain()
                        v.tensor_copy(widx[j % 4][:, :],
                                      tmpk[:, :]).then_inc(d_slotc, 1)

                    slotcast(0)
                    slotcast(1)
                    for rec in pes:
                        i = rec["i"]
                        cap = rec["cap"]
                        for fc in range(ND):
                            n = i * ND + fc
                            v.wait_ge(e_silu, n + 1)
                            v.wait_ge(e_peU, n + 1)
                            v.tensor_mul(ge_buf[:, fc * CAP:fc * CAP + cap],
                                         hs[n % 2][:, 0:cap], psU[:, 0:cap]
                                         ).then_inc(e_gmul, 1)
                        slotcast(i + 3)
                        if rec["routed"]:
                            v.wait_ge(sp_wg, 16 * (rec["wg0"] + rec["nt"]))
                        for r in range(rec["nt"]):
                            yidx = rec["ybase"] + r
                            v.wait_ge(e_peY, yidx + 1)
                            if rec["routed"]:
                                ridx = rec["rybase"] + r
                                if ridx >= 3:
                                    v.wait_ge(sp_sadd, 16 * (ridx - 2))
                                v.tensor_scalar(ystage[ridx % 3][:, :],
                                                psY[yidx % 2][:, :],
                                                wg_sb[i % 4][:, r:r + 1],
                                                None, OP.mult
                                                ).then_inc(e_yev, 1)
                            else:
                                tok = rec["q"] * 4 + r
                                sl_t = slice(tok * DM, (tok + 1) * DM)
                                v.scalar_tensor_tensor(
                                    acc[:, sl_t], psY[yidx % 2][:, :], 1.0,
                                    acc[:, sl_t], OP.mult, OP.add
                                    ).then_inc(e_yev, 1)
                blk.vector(ve)

                def sy(s):
                    cumx = 0
                    cumt = 0
                    for rec in pes:
                        i = rec["i"]
                        cumx += rec["nt"]
                        s.wait_ge(sp_xg, 16 * cumx)
                        if i >= 2:
                            s.wait_ge(sp_gdone, i - 1)
                        fm3 = x1e_fm[i % 2][:, :].rearrange(
                            "p (c f) -> p c f", c=ND)
                        for r in range(rec["nt"]):
                            s.dma_start_transpose(
                                fm3[:, :, r * 128:(r + 1) * 128],
                                x1e_tm[:, r * DM:(r + 1) * DM]
                                ).then_inc(sp_xts, 16)
                            cumt += 1
                            s.wait_ge(sp_xts, 16 * cumt)
                blk.sync(sy)
            emit_tap("g0", ge_buf[:, :])
        emit_tap("slotl", slotl_d[:, :])

        with ExitStack() as SF:
            sf_ = lambda name, shape, dt: SF.enter_context(
                nc.sbuf_tensor(name, list(shape), dt))
            ost = [sf_(f"ost{i}", (128, DM), F32) for i in range(2)]
            accr = [sf_(f"accr{i}", (128, 4 * DM), F32) for i in range(2)]
            bn12f = sf_("bn12f", (128, 12), F32)
            mvf = sf_("mvf", (128, 2), F32)
            rsqf = sf_("rsqf", (128, 1), F32)
            with nc.Block() as blk:
                def ve(v):
                    for ti in range(NT):
                        sl = slice(ti * DM, (ti + 1) * DM)
                        v.wait_ge(f_ld[ti % 2], 16 * 4 * (ti // 2 + 1))
                        for k in range(4):
                            v.tensor_add(acc[:, sl], acc[:, sl],
                                         accr[ti % 2][:, k * DM:(k + 1) * DM])
                            v.drain()
                        v.nop().then_inc(f_addc, 1)
                        v.bn_stats(bn12f[:, 0:6], acc[:, ti * DM:ti * DM + 512])
                        v.bn_stats(bn12f[:, 6:12],
                                   acc[:, ti * DM + 512:(ti + 1) * DM])
                        v.drain()
                        v.bn_aggr(mvf[:, :], bn12f[:, :]).then_inc(f_dve, 1)
                        v.wait_ge(f_act, ti + 1)
                        if ti >= 2:
                            v.wait_ge(f_out[ti % 2], 16 * (ti // 2))
                        v.tensor_scalar(ost[ti % 2][:, :], acc[:, sl],
                                        mvf[:, 0:1], rsqf[:, 0:1],
                                        OP.subtract, OP.mult).then_inc(f_dve, 1)
                blk.vector(ve)

                def ac(s):
                    for ti in range(NT):
                        s.wait_ge(f_dve, 2 * ti + 1)
                        s.activation(rsqf[:, :], mvf[:, 1:2], AF.Ln,
                                     bias=constb[:, 0:1])
                        s.drain()
                        s.activation(rsqf[:, :], rsqf[:, :], AF.Exp,
                                     scale=-0.5).then_inc(f_act, 1)
                blk.scalar(ac)

                def sy(s):
                    def fload(ti):
                        if ti >= 2:
                            s.wait_ge(f_addc, ti - 1)
                        for k in range(4):
                            r0 = k * 4096 + ti * 128
                            s.dma_start(accr[ti % 2][:, k * DM:(k + 1) * DM],
                                        yrank_d[r0:r0 + 128, :]
                                        ).then_inc(f_ld[ti % 2], 16)
                    s.wait_ge(sp_sadd, 16 * RYT)
                    fload(0)
                    fload(1)
                    for ti in range(NT):
                        if ti + 2 < NT:
                            fload(ti + 2)
                        s.wait_ge(f_dve, 2 * ti + 2)
                        s.dma_start(out_d[ti * 128:(ti + 1) * 128, :],
                                    ost[ti % 2][:, :]).then_inc(f_out[ti % 2], 16)
                    s.wait_ge(f_out[0], 16 * 8)
                    s.wait_ge(f_out[1], 16 * 8)
                blk.sync(sy)

    return nc


_CONSTS = None


def kernel(L0_attn_x, L0_attn_Wq, L0_attn_Wk, L0_attn_Wv, L0_attn_Wo,
           L0_attn_lambda_q1, L0_attn_lambda_k1, L0_attn_lambda_q2,
           L0_attn_lambda_k2, L0_ffn_router_w, L0_ffn_bias, L0_ffn_gate_w,
           L0_ffn_up_w, L0_ffn_down_w, L0_ffn_sh_gate_w, L0_ffn_sh_up_w,
           L0_ffn_sh_down_w):
    global _CONSTS
    if _CONSTS is None:
        _CONSTS = _host_consts()
    c = _CONSTS
    x = np.ascontiguousarray(np.asarray(L0_attn_x, dtype=np.float32))
    common = {
        "wq": np.ascontiguousarray(L0_attn_Wq, dtype=np.float32),
        "wk": np.ascontiguousarray(L0_attn_Wk, dtype=np.float32),
        "wv": np.ascontiguousarray(L0_attn_Wv, dtype=np.float32),
        "wo": np.ascontiguousarray(L0_attn_Wo, dtype=np.float32),
        "lam0": np.ascontiguousarray(L0_attn_lambda_q1, dtype=np.float32),
        "lam1": np.ascontiguousarray(L0_attn_lambda_k1, dtype=np.float32),
        "lam2": np.ascontiguousarray(L0_attn_lambda_q2, dtype=np.float32),
        "lam3": np.ascontiguousarray(L0_attn_lambda_k2, dtype=np.float32),
        "rw": np.ascontiguousarray(L0_ffn_router_w, dtype=np.float32),
        "rbias": np.ascontiguousarray(L0_ffn_bias, dtype=np.float32),
        "gw": np.ascontiguousarray(L0_ffn_gate_w, dtype=np.float32),
        "uw": np.ascontiguousarray(L0_ffn_up_w, dtype=np.float32),
        "dw": np.ascontiguousarray(L0_ffn_down_w, dtype=np.float32),
        "shg": np.ascontiguousarray(L0_ffn_sh_gate_w, dtype=np.float32),
        "shu": np.ascontiguousarray(L0_ffn_sh_up_w, dtype=np.float32),
        "shd": np.ascontiguousarray(L0_ffn_sh_down_w, dtype=np.float32),
        "aq": c["aq"], "bq": c["bq"], "ak": c["ak"], "bk": c["bk"],
        "perm": c["permT"], "ident": c["ident"],
        "tokid": c["tokid"], "eoff": c["eoff"],
    }
    nc = build_program(stage=4)
    in_maps = [dict(common, x=np.ascontiguousarray(x[b])) for b in range(8)]
    res = run_bass_kernel_spmd(nc, in_maps, core_ids=list(range(8)))
    out = np.stack([np.asarray(r["out"]) for r in res.results], axis=0)
    return out.astype(np.float32)


# version 23 (final)
# speedup vs baseline: 1.0110x; 1.0110x over previous
import math
from contextlib import ExitStack

import numpy as np

import concourse.bass as bass
import concourse.mybir as mybir
from concourse.alu_op_type import AluOpType as OP
from concourse.bass_utils import run_bass_kernel_spmd
from concourse.bass_core_rust import IndirectOffsetOnAxis

F32 = mybir.dt.float32
BF16 = mybir.dt.bfloat16
AF = mybir.ActivationFunctionType
AX = mybir.AxisListType

T = 2048
DM = 1024
NT = 16
ND = 8
NP = 8
NK = 16
NE = 18
LAMBDA_INIT = 0.8 - 0.6 * math.exp(-0.3)
LN_EPS = 1e-5
RMS_BIAS = math.log(1.0 - LAMBDA_INIT)
ATTN_PAIRS = NP
CAP = 768
DUMP = 16 * CAP


def _host_consts():
    D = 64
    inv_freq = 1.0 / 10000.0 ** (np.arange(0, D, 2, dtype=np.float64) / D)
    pos = np.arange(T, dtype=np.float64)
    freqs = pos[:, None] * inv_freq[None, :]
    freqs = np.repeat(freqs, 2, axis=-1)
    sv = (np.arange(0, D, 2, dtype=np.float64) + 0.4 * D) / (1.4 * D)
    power = (pos - T // 2) / 512.0
    scale = sv[None, :] ** power[:, None]
    scale = np.concatenate([scale, scale], axis=-1)

    def fm(m):
        return np.tile(m.T, (2, 1)).astype(np.float32)

    aq = fm(np.cos(freqs) * scale / 8.0)
    bq = fm(np.sin(freqs) * scale / 8.0)
    ak = fm(np.cos(freqs) / scale)
    bk = fm(np.sin(freqs) / scale)
    m = np.zeros((128, 128), dtype=np.float32)
    for blk in (0, 64):
        for a in range(32):
            m[blk + 2 * a, blk + 2 * a + 1] = -1.0
            m[blk + 2 * a + 1, blk + 2 * a] = 1.0
    tokid = np.zeros((128, 16), dtype=np.float32)
    for t in range(16):
        tokid[:, t] = t * 128 + np.arange(128)
    eoff = np.zeros((128, 1), dtype=np.float32)
    for e in range(16):
        eoff[e, 0] = e * 6
    return {"aq": aq, "bq": bq, "ak": ak, "bk": bk, "permT": m.T.copy(),
            "ident": np.eye(128, dtype=np.float32),
            "tokid": tokid, "eoff": eoff}


def build_program(stage=4, taps=()):
    nc = bass.Bass("TRN2", target_bir_lowering=False)
    taps = set(taps)

    def din(name, shape):
        return nc.dram_tensor(name, list(shape), F32, kind="ExternalInput").ap()

    x_d = din("x", (T, DM))
    wq_d = din("wq", (DM, DM))
    wk_d = din("wk", (DM, DM))
    wv_d = din("wv", (DM, DM))
    wo_d = din("wo", (DM, DM))
    lam_d = [din(f"lam{i}", (64,)) for i in range(4)]
    rw_d = din("rw", (16, DM))
    rbias_d = din("rbias", (16,))
    gw_d = din("gw", (16, DM, DM))
    uw_d = din("uw", (16, DM, DM))
    dw_d = din("dw", (16, DM, DM))
    shg_d = din("shg", (DM, 2048))
    shu_d = din("shu", (DM, 2048))
    shd_d = din("shd", (2048, DM))
    aq_d = din("aq", (128, T))
    bq_d = din("bq", (128, T))
    ak_d = din("ak", (128, T))
    bk_d = din("bk", (128, T))
    perm_d = din("perm", (128, 128))
    ident_d = din("ident", (128, 128))
    tokid_d = din("tokid", (128, 16))
    eoff_d = din("eoff", (128, 1))
    out_d = nc.dram_tensor("out", [T, DM], F32, kind="ExternalOutput").ap()
    crs_hi = nc.dram_tensor("crs_hi", [32, 128], BF16, kind="Internal").ap()
    crs_lo = nc.dram_tensor("crs_lo", [32, 128], BF16, kind="Internal").ap()
    x1t_d = nc.dram_tensor("x1t", [T + 1, DM], BF16, kind="Internal").ap()
    cmbd_d = nc.dram_tensor("cmbd", [T + 1, 16], F32, kind="Internal").ap()
    slotl_d = nc.dram_tensor("slotl", [DUMP + 1, 1], F32, kind="Internal").ap()
    yrank_d = nc.dram_tensor("yrank", [3 * 4096 + T, DM], F32,
                             kind="Internal").ap()
    gwb_d = nc.dram_tensor("gwb", [16, DM, DM], BF16, kind="Internal").ap()
    uwb_d = nc.dram_tensor("uwb", [16, DM, DM], BF16, kind="Internal").ap()
    dwb_d = nc.dram_tensor("dwb", [16, DM, DM], BF16, kind="Internal").ap()
    shgb_d = nc.dram_tensor("shgb", [DM, 2048], BF16, kind="Internal").ap()
    shub_d = nc.dram_tensor("shub", [DM, 2048], BF16, kind="Internal").ap()
    shdb_d = nc.dram_tensor("shdb", [2048, DM], BF16, kind="Internal").ap()
    xb_d = nc.dram_tensor("xb", [T, DM], BF16, kind="Internal").ap()
    wqb2_d = nc.dram_tensor("wqb2", [DM, DM], BF16, kind="Internal").ap()
    wkb2_d = nc.dram_tensor("wkb2", [DM, DM], BF16, kind="Internal").ap()
    wvb2_d = nc.dram_tensor("wvb2", [DM, DM], BF16, kind="Internal").ap()
    wob2_d = nc.dram_tensor("wob2", [DM, DM], BF16, kind="Internal").ap()

    tap_shapes = {"qall": (128, NP * T), "kall": (128, NP * T),
                  "v": (128, NT * DM), "xfm": (128, ND * T),
                  "attnfm": (128, NP * T), "acc": (128, NT * DM),
                  "x1fm": (128, ND * T), "cmb": (128, NT * 16),
                  "crow": (1, 2 * T), "crowlo": (1, 2 * T), "sa": (128, 32), "g0": (128, ND * CAP),
                  "ut": (128, T), "ptb1": (128, T), "nmx": (128, 32),
                  "lnS": (128, 32), "chi": (128, 32), "clo": (128, 32),
                  "thi": (32, 128), "tlo": (32, 128), "cf": (128, 32),
                  "postm": (128, 256), "slotl": (DUMP + 1, 1),
                  "accd": (T + 1, DM), "selst": (128, 256)}
    tap_d = {}
    for t_ in taps:
        dt_ = F32 if t_ in ("acc", "cmb", "sa", "ut", "nmx", "lnS", "cf",
                            "postm", "slotl", "accd", "selst") else BF16
        tap_d[t_] = nc.dram_tensor("tap_" + t_, list(tap_shapes[t_]), dt_,
                                   kind="ExternalOutput").ap()

    with ExitStack() as S:
        sb = lambda name, shape, dt, **kw: S.enter_context(
            nc.sbuf_tensor(name, list(shape), dt, **kw))
        sem = lambda name: S.enter_context(nc.semaphore(name))

        permT_b = sb("permT", (128, 128), BF16)
        ident_b = sb("identb", (128, 128), BF16)
        onesrow_bf = sb("onesrow_bf", (1, 128), BF16)
        onescol_bf = sb("onescol_bf", (128, 1), BF16)
        onesrow_f = sb("onesrow_f", (1, 128), F32)
        ones4_f = sb("ones4_f", (128, 4), F32)
        neglam = sb("neglam", (128, 1), F32)
        routerT = sb("routerT", (128, 128), BF16)
        biasbc = sb("biasbc", (128, 16), F32)
        lamrow = sb("lamrow", (1, 256), F32)
        lsc = sb("lsc", (1, 96), F32)
        cmb = sb("cmb", (128, NT * 16), F32)
        constb = sb("constb", (128, 2), F32)
        SR_ = S.enter_context(ExitStack())
        fmbuf = SR_.enter_context(
            nc.sbuf_tensor("fmbuf", [128, ND * T], BF16, side="right"))
        woT = SR_.enter_context(
            nc.sbuf_tensor("woT", [128, ND * DM], BF16, side="right"))

        ldx = sem("ldx")
        ldpi = sem("ldpi")
        ldwo = sem("ldwo")
        ldrt = sem("ldrt")
        ldw = sem("ldw")
        ldtab = sem("ldtab")
        ldmisc = sem("ldmisc")
        a_pe = sem("a_pe")
        a_dve = sem("a_dve")
        a_act = sem("a_act")
        b_pe = sem("b_pe")
        b_dve = sem("b_dve")
        b_pex = sem("b_pex")
        b_dvex = sem("b_dvex")
        b_peQ = sem("b_peQ")
        b_actQ = sem("b_actQ")
        b_peR = sem("b_peR")
        b_dveQ = sem("b_dveQ")
        v_pe = sem("v_pe")
        v_act = sem("v_act")
        c_peS = sem("c_peS")
        c_max = sem("c_max")
        c_s1a = sem("c_s1a")
        c_s1b = sem("c_s1b")
        c_s1c = sem("c_s1c")
        c_s1d = sem("c_s1d")
        c_sumc = sem("c_sumc")
        c_s2a = sem("c_s2a")
        c_s2b = sem("c_s2b")
        c_epi = sem("c_epi")
        c_lnS = sem("c_lnS")
        c_crow = sem("c_crow")
        c_ctr = sem("c_ctr")
        c_scr = sem("c_scr")
        c_cmat = sem("c_cmat")
        c_peCT = sem("c_peCT")
        c_aug = sem("c_aug")
        c_crowD = sem("c_crowD")
        c_peST = sem("c_peST")
        c_pv = sem("c_pv")
        c_ep1 = sem("c_ep1")
        c_ep2 = sem("c_ep2")
        c_ep3 = sem("c_ep3")
        c_ep4 = sem("c_ep4")
        c_attnD = sem("c_attnD")
        d_xld = [sem("d_xld0"), sem("d_xld1")]
        d_peO = sem("d_peO")
        d_add = sem("d_add")
        d_bn = sem("d_bn")
        d_act = sem("d_act")
        d_x1b = sem("d_x1b")
        d_peT = sem("d_peT")
        d_evT = sem("d_evT")
        r_pe = sem("r_pe")
        r_act = sem("r_act")
        r_dve = sem("r_dve")
        e_wld = [sem(f"e_wld{i}") for i in range(4)]
        e_wdone = sem("e_wdone")
        e_peH = sem("e_peH")
        e_peU = sem("e_peU")
        e_silu = sem("e_silu")
        e_gmul = sem("e_gmul")
        e_peY = sem("e_peY")
        e_yev = sem("e_yev")
        sp_z = sem("sp_z")
        sp_wcast = sem("sp_wcast")
        sp_pcast = sem("sp_pcast")
        sp_zacc = sem("sp_zacc")
        sp_cmb = sem("sp_cmb")
        sp_dmisc = sem("sp_dmisc")
        sp_x1d = [sem("sp_x1d0"), sem("sp_x1d1")]
        d_selT = sem("d_selT")
        d_selE = sem("d_selE")
        d_pos = sem("d_pos")
        d_posT = sem("d_posT")
        d_posE = sem("d_posE")
        sp_scat = sem("sp_scat")
        sp_slotl = sem("sp_slotl")
        d_slotc = sem("d_slotc")
        sp_xg = sem("sp_xg")
        sp_xts = sem("sp_xts")
        sp_gdone = sem("sp_gdone")
        sp_wg = sem("sp_wg")
        sp_sadd = sem("sp_sadd")
        f_dve = sem("f_dve")
        f_act = sem("f_act")
        f_out = [sem("f_out0"), sem("f_out1")]
        f_ld = [sem("f_ld0"), sem("f_ld1")]
        f_addc = sem("f_addc")
        tapsem = sem("tapsem")
        tap_count = [0]

        def emit_tap(name, ap):
            if name not in taps:
                return
            with nc.Block() as blk:
                def body(sy, ap=ap, name=name):
                    sy.dma_start(tap_d[name][:, :], ap).then_inc(tapsem, 16)
                    tap_count[0] += 16
                    sy.wait_ge(tapsem, tap_count[0])
                blk.sync(body)

        S1 = S.enter_context(ExitStack())
        sb1 = lambda name, shape, dt: S1.enter_context(
            nc.sbuf_tensor(name, list(shape), dt))
        vbuf = sb1("vbuf", (128, NT * DM), BF16)
        qall = sb1("qall", (128, NP * T), BF16)
        kall = sb1("kall", (128, NP * T), BF16)
        S1w = S.enter_context(ExitStack())
        wstage = S1w.enter_context(
            nc.sbuf_tensor("wstage", [128, ND * DM], BF16))

        with ExitStack() as SA:
            rstage = SA.enter_context(nc.sbuf_tensor("rstage", [16, DM], BF16))
            pbf = SA.enter_context(nc.psum_tensor("pbf", [128, DM], BF16))
            psf = SA.enter_context(nc.psum_tensor("psf", [128, 512], F32))
            with nc.Block() as blk:
                def gp(g):
                    def flat(ap):
                        return ap.rearrange("a b -> (a b)").unsqueeze(0)
                    for src, dst in ((wo_d, wob2_d), (wq_d, wqb2_d),
                                     (x_d, xb_d), (wk_d, wkb2_d),
                                     (wv_d, wvb2_d)):
                        g.dma_start(flat(dst[:, :]), flat(src[:, :])
                                    ).then_inc(sp_pcast, 16)
                    g.dma_start(permT_b[:, :], perm_d[:, :]).then_inc(ldpi, 16)
                    g.dma_start(ident_b[:, :], ident_d[:, :]).then_inc(ldpi, 16)
                    g.dma_start(rstage[:, :], rw_d[:, :]).then_inc(ldrt, 16)
                blk.gpsimd(gp)

                def sy(s):
                    for i in range(4):
                        s.dma_start(lamrow[0:1, i * 64:(i + 1) * 64],
                                    lam_d[i].unsqueeze(0)).then_inc(ldmisc, 16)
                    s.dma_start(lsc[0:1, 80:96],
                                rbias_d.unsqueeze(0)).then_inc(ldmisc, 16)
                    w3d = wstage[:, :].rearrange("p (c f) -> p c f", c=ND)
                    s.wait_ge(sp_pcast, 16)
                    s.dma_start(w3d, wob2_d.rearrange("(c p) f -> p c f",
                                                      p=128)).then_inc(ldwo, 16)
                    s.wait_ge(a_pe, 8)
                    s.wait_ge(sp_pcast, 32)
                    s.dma_start(w3d, wqb2_d.rearrange("(c p) f -> p c f",
                                                      p=128)).then_inc(ldw, 16)
                    s.wait_ge(sp_pcast, 48)
                    x3s = xb_d.rearrange("(c p) f -> p c f", p=128)
                    x3d = vbuf[:, :].rearrange("p (c f) -> p c f", c=NT)
                    s.dma_start(x3d, x3s).then_inc(ldx, 16)
                blk.sync(sy)

                def pe(t):
                    t.wait_ge(ldwo, 16)
                    t.wait_ge(ldpi, 32)
                    for dc in range(ND):
                        if dc >= 1:
                            t.wait_ge(a_dve, dc)
                        last = None
                        for ic in range(ND):
                            last = t.transpose(
                                pbf[:, ic * 128:(ic + 1) * 128],
                                wstage[:, ic * DM + dc * 128:
                                       ic * DM + dc * 128 + 128],
                                ident_b[:, :])
                        last.then_inc(a_pe, 1)
                    t.wait_ge(ldrt, 16)
                    t.wait_ge(a_dve, 8)
                    last = None
                    for dc in range(ND):
                        last = t.transpose(pbf[:, dc * 16:dc * 16 + 16],
                                           rstage[0:16, dc * 128:(dc + 1) * 128],
                                           ident_b[0:16, 0:16])
                    last.then_inc(a_pe, 1)
                    t.wait_ge(a_dve, 11)
                    t.matmul(psf[:, 0:1], onesrow_f[0:1, :], lsc[0:1, 68:69],
                             start=True, stop=True).then_inc(a_pe, 1)
                    t.wait_ge(a_dve, 12)
                    t.matmul(psf[:, 0:16], onesrow_f[0:1, :], lsc[0:1, 80:96],
                             start=True, stop=True).then_inc(a_pe, 1)
                blk.tensor(pe)

                def ve(v):
                    v.memset(onesrow_bf[:, :], 1.0)
                    v.memset(onescol_bf[:, :], 1.0)
                    v.memset(onesrow_f[:, :], 1.0)
                    v.memset(ones4_f[:, :], 1.0)
                    v.memset(constb[:, 0:1], LN_EPS)
                    v.memset(constb[:, 1:2], RMS_BIAS)
                    for dc in range(ND):
                        v.wait_ge(a_pe, dc + 1)
                        v.tensor_copy(woT[:, dc * DM:(dc + 1) * DM],
                                      pbf[:, :]).then_inc(a_dve, 1)
                    v.wait_ge(a_pe, 9)
                    v.tensor_copy(routerT[:, :], pbf[:, 0:128]).then_inc(a_dve, 1)
                    v.wait_ge(ldmisc, 80)
                    v.tensor_mul(lsc[0:1, 0:64], lamrow[0:1, 0:64],
                                 lamrow[0:1, 64:128])
                    v.reduce_sum(lsc[0:1, 64:65], lsc[0:1, 0:64], axis=AX.X)
                    v.tensor_mul(lsc[0:1, 0:64], lamrow[0:1, 128:192],
                                 lamrow[0:1, 192:256])
                    v.reduce_sum(lsc[0:1, 65:66], lsc[0:1, 0:64],
                                 axis=AX.X).then_inc(a_dve, 1)
                    v.wait_ge(a_act, 1)
                    v.tensor_sub(lsc[0:1, 68:69], lsc[0:1, 66:67], lsc[0:1, 67:68])
                    v.drain()
                    v.tensor_scalar_add(lsc[0:1, 68:69], lsc[0:1, 68:69],
                                        LAMBDA_INIT).then_inc(a_dve, 1)
                    v.wait_ge(a_pe, 10)
                    v.tensor_scalar_mul(neglam[:, :], psf[:, 0:1],
                                        -1.0).then_inc(a_dve, 1)
                    v.wait_ge(a_pe, 11)
                    v.tensor_copy(biasbc[:, :], psf[:, 0:16])
                blk.vector(ve)

                def ac(s):
                    s.wait_ge(a_dve, 10)
                    s.activation(lsc[0:1, 66:68], lsc[0:1, 64:66],
                                 AF.Exp).then_inc(a_act, 1)
                blk.scalar(ac)

        with ExitStack() as SB:
            pxf = [SB.enter_context(nc.psum_tensor(f"pxf{i}", [128, DM], BF16))
                   for i in range(2)]
            with nc.Block() as blk:
                def pe(t):
                    t.wait_ge(ldx, 16)
                    for ti in range(NT):
                        pp = pxf[ti % 2]
                        if ti >= 2:
                            t.wait_ge(b_dvex, ti - 1)
                        last = None
                        for dc in range(ND):
                            last = t.transpose(
                                pp[:, dc * 128:(dc + 1) * 128],
                                vbuf[:, ti * DM + dc * 128:
                                     ti * DM + dc * 128 + 128],
                                ident_b[:, :])
                        last.then_inc(b_pex, 1)
                blk.tensor(pe)

                def ve(v):
                    for ti in range(NT):
                        pp = pxf[ti % 2]
                        v.wait_ge(b_pex, ti + 1)
                        d3 = fmbuf[:, :].rearrange("p (c f) -> p c f", c=ND)
                        v.tensor_copy(d3[:, :, ti * 128:ti * 128 + 128],
                                      pp[:, :].rearrange("p (c f) -> p c f",
                                                         c=ND)
                                      ).then_inc(b_dvex, 1)
                blk.vector(ve)

        emit_tap("xfm", fmbuf[:, :])

        def proj_rope(which):
            dst = qall if which == "q" else kall
            base = {"q": 0, "k": 1}[which]
            b0 = base * 8
            with ExitStack() as SS:
                wT = SS.enter_context(
                    nc.sbuf_tensor(f"wT_{which}", [128, ND * DM], BF16))
                with ExitStack() as ST:
                    pwt = [ST.enter_context(nc.psum_tensor(
                        f"pwt{i}_{which}", [128, DM], BF16)) for i in range(2)]
                    with nc.Block() as blk:
                        def pe(t):
                            t.wait_ge(ldw, 16 * (base + 1))
                            for dc in range(ND):
                                pp = pwt[dc % 2]
                                if dc >= 2:
                                    t.wait_ge(b_dve, b0 + dc - 1)
                                last = None
                                for ic in range(ND):
                                    last = t.transpose(
                                        pp[:, ic * 128:(ic + 1) * 128],
                                        wstage[:, ic * DM + dc * 128:
                                               ic * DM + dc * 128 + 128],
                                        ident_b[:, :])
                                last.then_inc(b_pe, 1)
                        blk.tensor(pe)

                        def ve(v):
                            for dc in range(ND):
                                pp = pwt[dc % 2]
                                v.wait_ge(b_pe, b0 + dc + 1)
                                v.tensor_copy(wT[:, dc * DM:(dc + 1) * DM],
                                              pp[:, :]).then_inc(b_dve, 1)
                        blk.vector(ve)
                with ExitStack() as SP:
                    tabA = SP.enter_context(
                        nc.sbuf_tensor(f"tabA_{which}", [128, T], BF16))
                    tabB = SP.enter_context(
                        nc.sbuf_tensor(f"tabB_{which}", [128, T], BF16))
                    qsb = SP.enter_context(
                        nc.sbuf_tensor(f"qsb_{which}", [128, T], BF16))
                    tmpf = SP.enter_context(
                        nc.sbuf_tensor(f"tmpf_{which}", [128, T], F32))
                    pq = SP.enter_context(
                        nc.psum_tensor(f"pq_{which}", [128, T], F32))
                    prot = SP.enter_context(
                        nc.psum_tensor(f"prot_{which}", [128, T], F32))
                    with nc.Block() as blk:
                        def gp(g):
                            srcA = aq_d if which == "q" else ak_d
                            srcB = bq_d if which == "q" else bk_d
                            g.dma_start(tabA[:, :], srcA[:, :]).then_inc(ldtab, 16)
                            g.dma_start(tabB[:, :], srcB[:, :]).then_inc(ldtab, 16)
                        blk.gpsimd(gp)

                        def syw(s):
                            nxt = wkb2_d if which == "q" else wvb2_d
                            s.wait_ge(sp_pcast, 64 if which == "q" else 80)
                            w3s = nxt.rearrange("(c p) f -> p c f", p=128)
                            w3d = wstage[:, :].rearrange("p (c f) -> p c f",
                                                         c=ND)
                            s.dma_start(w3d, w3s).then_inc(ldw, 16)
                        blk.sync(syw)

                        def pe(t):
                            for g in range(ND):
                                gb = b0 + g
                                if g >= 1:
                                    t.wait_ge(b_actQ, gb)
                                last = None
                                for nch in range(4):
                                    for dc in range(ND):
                                        last = t.matmul(
                                            pq[:, nch * 512:(nch + 1) * 512],
                                            wT[:, dc * DM + g * 128:
                                               dc * DM + g * 128 + 128],
                                            fmbuf[:, dc * T + nch * 512:
                                                  dc * T + (nch + 1) * 512],
                                            start=(dc == 0),
                                            stop=(dc == ND - 1))
                                last.then_inc(b_peQ, 1)
                                t.wait_ge(b_actQ, gb + 1)
                                if g >= 1:
                                    t.wait_ge(b_dveQ, gb)
                                last = None
                                for nch in range(4):
                                    last = t.matmul(
                                        prot[:, nch * 512:(nch + 1) * 512],
                                        permT_b[:, :],
                                        qsb[:, nch * 512:(nch + 1) * 512],
                                        start=True, stop=True)
                                last.then_inc(b_peR, 1)
                        blk.tensor(pe)

                        def ac(s):
                            for g in range(ND):
                                gb = b0 + g
                                s.wait_ge(b_peQ, gb + 1)
                                if g >= 1:
                                    s.wait_ge(b_dveQ, gb)
                                s.activation(qsb[:, :], pq[:, :],
                                             AF.Copy).then_inc(b_actQ, 1)
                        blk.scalar(ac)

                        def ve(v):
                            v.wait_ge(ldtab, 32 * (base + 1))
                            for g in range(ND):
                                gb = b0 + g
                                v.wait_ge(b_actQ, gb + 1)
                                v.tensor_mul(tmpf[:, :], qsb[:, :], tabA[:, :])
                                v.wait_ge(b_peR, gb + 1)
                                v.tensor_mul(qsb[:, :], prot[:, :], tabB[:, :])
                                v.tensor_add(dst[:, g * T:(g + 1) * T],
                                             tmpf[:, :],
                                             qsb[:, :]).then_inc(b_dveQ, 1)
                        blk.vector(ve)

        proj_rope("q")
        proj_rope("k")
        emit_tap("qall", qall[:, :])
        emit_tap("kall", kall[:, :])

        with ExitStack() as SV:
            wT = SV.enter_context(nc.sbuf_tensor("wT_v", [128, ND * DM], BF16))
            pwt = [SV.enter_context(nc.psum_tensor(f"pwtv{i}", [128, DM], BF16))
                   for i in range(2)]
            pv = [SV.enter_context(nc.psum_tensor(f"pv{i}", [128, DM], F32))
                  for i in range(2)]
            with nc.Block() as blk:
                def pe(t):
                    t.wait_ge(ldw, 48)
                    for dc in range(ND):
                        pp = pwt[dc % 2]
                        if dc >= 2:
                            t.wait_ge(b_dve, 16 + dc - 1)
                        last = None
                        for ic in range(ND):
                            last = t.transpose(
                                pp[:, ic * 128:(ic + 1) * 128],
                                wstage[:, ic * DM + dc * 128:
                                       ic * DM + dc * 128 + 128],
                                ident_b[:, :])
                        last.then_inc(b_pe, 1)
                    t.wait_ge(b_dve, 24)
                    for ti in range(NT):
                        pp = pv[ti % 2]
                        if ti >= 2:
                            t.wait_ge(v_act, ti - 1)
                        last = None
                        for nch in range(2):
                            for dc in range(ND):
                                last = t.matmul(
                                    pp[:, nch * 512:(nch + 1) * 512],
                                    fmbuf[:, dc * T + ti * 128:
                                          dc * T + ti * 128 + 128],
                                    wT[:, dc * DM + nch * 512:
                                       dc * DM + (nch + 1) * 512],
                                    start=(dc == 0), stop=(dc == ND - 1))
                        last.then_inc(v_pe, 1)
                blk.tensor(pe)

                def ve(v):
                    for dc in range(ND):
                        pp = pwt[dc % 2]
                        v.wait_ge(b_pe, 16 + dc + 1)
                        v.tensor_copy(wT[:, dc * DM:(dc + 1) * DM],
                                      pp[:, :]).then_inc(b_dve, 1)
                blk.vector(ve)

                def ac(s):
                    for ti in range(NT):
                        pp = pv[ti % 2]
                        s.wait_ge(v_pe, ti + 1)
                        s.activation(vbuf[:, ti * DM:(ti + 1) * DM], pp[:, :],
                                     AF.Copy).then_inc(v_act, 1)
                blk.scalar(ac)
        emit_tap("v", vbuf[:, :])
        S1w.close()
        if stage < 2:
            return nc

        with ExitStack() as SC:
            sc_ = lambda name, shape, dt: SC.enter_context(
                nc.sbuf_tensor(name, list(shape), dt))
            pt = [sc_(f"pt{i}", (128, T), BF16) for i in range(4)]
            vpp = sc_("vpp", (128, T), BF16)
            nmx = sc_("nmx", (128, 32), F32)
            nm2 = sc_("nm2", (128, 64), F32)
            sa2 = sc_("sa2", (128, 64), F32)
            sacc = sc_("sacc", (128, 32), F32)
            lnS = sc_("lnS", (128, 32), F32)
            cf = sc_("cf", (128, 32), F32)
            chi = sc_("chi", (128, 32), BF16)
            clo = sc_("clo", (128, 32), BF16)
            thi = sc_("thi", (32, 128), BF16)
            tlo = sc_("tlo", (32, 128), BF16)
            crow2 = sc_("crow2", (2, 2 * T), BF16)
            kaug = sc_("kaug", (128, T), BF16)
            qaug = sc_("qaug", (128, T), BF16)
            lnrow = sc_("lnrow", (1, T), F32)
            utb = sc_("utb", (128, T), F32) if "ut" in taps else None
            rrow = sc_("rrow", (1, T), BF16)
            scrap = pt[0]
            rbc = pt[1]
            psA = SC.enter_context(nc.psum_tensor("psA", [128, T], F32))
            attnT = SC.enter_context(nc.psum_tensor("attnT", [128, T], F32))
            slots = [psA[:, 0:1024], psA[:, 1024:2048],
                     attnT[:, 0:1024], attnT[:, 1024:2048]]
            s1sem = (c_s1a, c_s1b, c_s1c, c_s1d)
            s2sem = (c_s2a, c_s2b)
            pthalf = [pt[i][:, 0:1024] for i in range(4)]

            def jdec(j):
                return (j % 32) // 2, j // 32, j % 2

            with nc.Block() as blk:
                def gp(g):
                    def flat(ap):
                        return ap.rearrange("a b -> (a b)").unsqueeze(0)

                    def cast(m):
                        if m < 48:
                            e, k = divmod(m, 3)
                            src = (gw_d, uw_d, dw_d)[k][e, :, :]
                            dst = (gwb_d, uwb_d, dwb_d)[k][e, :, :]
                            g.dma_start(flat(dst), flat(src)
                                        ).then_inc(sp_wcast, 16)
                        elif m < 51:
                            src, dst = ((shg_d, shgb_d), (shu_d, shub_d),
                                        (shd_d, shdb_d))[m - 48]
                            g.dma_start(flat(dst[:, :]), flat(src[:, :])
                                        ).then_inc(sp_wcast, 32)

                    cast_ord = [48, 49, 50] + list(range(48))
                    nc_per = (51 + ATTN_PAIRS - 1) // ATTN_PAIRS
                    for h in range(ATTN_PAIRS):
                        g.wait_ge(c_ctr, h + 1)
                        g.dma_start(crs_hi[:, :], thi[:, :]).then_inc(c_scr, 16)
                        g.dma_start(crs_lo[:, :], tlo[:, :]).then_inc(c_scr, 16)
                        g.wait_ge(c_scr, 32 * (h + 1))
                        for hd in range(2):
                            fh = crs_hi[hd * 16:(hd + 1) * 16, :].rearrange(
                                "a b -> (a b)").unsqueeze(0)
                            fl = crs_lo[hd * 16:(hd + 1) * 16, :].rearrange(
                                "a b -> (a b)").unsqueeze(0)
                            g.dma_start(crow2[0:1, hd * T:(hd + 1) * T],
                                        fh).then_inc(c_crowD, 16)
                            g.dma_start(crow2[1:2, hd * T:(hd + 1) * T],
                                        fl).then_inc(c_crowD, 16)
                        for m in cast_ord[h * nc_per:(h + 1) * nc_per]:
                            cast(m)
                blk.gpsimd(gp)

                def pe(t):
                    t.wait_ge(a_dve, 1)
                    for h in range(ATTN_PAIRS):
                        qc = h * T
                        for u in range(64):
                            k, sh = u // 2, u % 2
                            ti, hd = k // 2, k % 2
                            r = u % 4
                            if u == 0:
                                t.wait_ge(c_epi, h)
                            t.wait_ge(s1sem[r], h * 16 + u // 4)
                            rows = slice(hd * 64, hd * 64 + 64)
                            last = None
                            for nch in range(2):
                                last = t.matmul(
                                    slots[r][:, nch * 512:(nch + 1) * 512],
                                    qall[rows, qc + ti * 128: qc + ti * 128 + 128],
                                    kall[rows, qc + sh * 1024 + nch * 512:
                                         qc + sh * 1024 + (nch + 1) * 512],
                                    start=True, stop=True)
                            last.then_inc(c_peS, 1)
                        for ss in s1sem:
                            t.wait_ge(ss, h * 16 + 16)
                        t.wait_ge(c_cmat, h + 1)
                        t.transpose(psA[0:32, 0:64].bitcast(BF16), chi[:, :],
                                    ident_b[:, :])
                        t.transpose(psA[0:32, 64:128].bitcast(BF16), clo[:, :],
                                    ident_b[:, :]).then_inc(c_peCT, 1)
                        t.wait_ge(c_crow, h + 1)
                        t.wait_ge(c_crowD, 64 * (h + 1))
                        def pv_round(j):
                            kc, hd, qh = jdec(j)
                            t.wait_ge(s2sem[qh], h * 32 + j // 2 + 1)
                            if hd == 0:
                                lhsT = vbuf[:, kc * DM + h * 128:
                                            kc * DM + h * 128 + 128]
                            else:
                                lhsT = vpp[:, kc * 128:(kc + 1) * 128]
                            last = None
                            for nch in range(2):
                                last = t.matmul(
                                    attnT[:, qh * 1024 + nch * 512:
                                          qh * 1024 + (nch + 1) * 512],
                                    lhsT,
                                    pthalf[j % 4][:, nch * 512:(nch + 1) * 512],
                                    start=(kc == 0 and hd == 0),
                                    stop=(kc == NK - 1 and hd == 1))
                            last.then_inc(c_pv, 1)

                        for j in range(64):
                            kc, hd, qh = jdec(j)
                            if j % 32 == 0:
                                t.wait_ge(c_aug, 2 * h + hd + 1)
                            half = psA[:, qh * 1024:(qh + 1) * 1024]
                            t.wait_ge(s2sem[qh], h * 32 + j // 2)
                            last = None
                            for nch in range(2):
                                last = t.matmul(
                                    half[:, nch * 512:(nch + 1) * 512],
                                    kaug[:, kc * 128:(kc + 1) * 128],
                                    qaug[:, qh * 1024 + nch * 512:
                                         qh * 1024 + (nch + 1) * 512],
                                    start=True, stop=True)
                            last.then_inc(c_peST, 1)
                            if j >= 1:
                                pv_round(j - 1)
                        pv_round(63)
                        t.wait_ge(c_ep1, h + 1)
                        t.wait_ge(c_s2a, h * 32 + 32)
                        t.wait_ge(c_s2b, h * 32 + 32)
                        last = None
                        for nch in range(4):
                            last = t.matmul(psA[0:1, nch * 512:(nch + 1) * 512],
                                            onescol_bf[:, :],
                                            scrap[:, nch * 512:(nch + 1) * 512],
                                            start=True, stop=True)
                        last.then_inc(c_ep2, 1)
                        t.wait_ge(c_ep3, h + 1)
                        last = None
                        for nch in range(4):
                            last = t.matmul(psA[:, nch * 512:(nch + 1) * 512],
                                            onesrow_bf[0:1, :],
                                            rrow[0:1, nch * 512:(nch + 1) * 512],
                                            start=True, stop=True)
                        last.then_inc(c_ep4, 1)
                blk.tensor(pe)

                def ve(v):
                    v.memset(kaug[64:128, :], 0.0)
                    v.memset(kaug[64:66, :], 1.0)
                    v.memset(qaug[64:128, :], 0.0)

                    def augbuild(h, hd):
                        v.wait_ge(c_peST, h * 64 + hd * 32)
                        v.wait_ge(c_crowD, 64 * h + 32 * (hd + 1))
                        rows = slice(hd * 64, hd * 64 + 64)
                        v.tensor_copy(kaug[0:64, :], kall[rows, h * T:(h + 1) * T])
                        v.tensor_copy(qaug[0:64, :], qall[rows, h * T:(h + 1) * T])
                        v.tensor_copy(qaug[64:66, :],
                                      crow2[0:2, hd * T:(hd + 1) * T]
                                      ).then_inc(c_aug, 1)

                    for h in range(ATTN_PAIRS):
                        for u in range(64):
                            k, sh = u // 2, u % 2
                            ti, hd = k // 2, k % 2
                            r = u % 4
                            cc = hd * 16 + ti
                            v.wait_ge(c_peS, h * 64 + u + 1)
                            inst = v.reduce_max(nm2[:, 2 * cc + sh:2 * cc + sh + 1],
                                                slots[r][:, :], axis=AX.X,
                                                negate=True)
                            if sh == 0:
                                inst.then_inc(c_max, 1)
                            else:
                                v.drain()
                                v.tensor_tensor(nmx[:, cc:cc + 1],
                                                nm2[:, 2 * cc:2 * cc + 1],
                                                nm2[:, 2 * cc + 1:2 * cc + 2],
                                                OP.min).then_inc(c_max, 1)
                        for ss in s1sem:
                            v.wait_ge(ss, h * 16 + 16)
                        v.tensor_add(sacc[:, :], sa2[:, 0:64:2],
                                     sa2[:, 1:64:2]).then_inc(c_sumc, 1)
                        v.wait_ge(c_lnS, h + 1)
                        if h >= 1:
                            v.wait_ge(c_crowD, 64 * h)
                        v.tensor_sub(cf[:, :], nmx[:, :], lnS[:, :])
                        v.drain()
                        v.tensor_copy(chi[:, :], cf[:, :])
                        v.drain()
                        v.tensor_sub(clo[:, :], cf[:, :],
                                     chi[:, :]).then_inc(c_cmat, 1)
                        v.wait_ge(c_peCT, h + 1)
                        v.tensor_copy(thi[:, :], psA[0:32, 0:64].bitcast(BF16))
                        v.tensor_copy(tlo[:, :], psA[0:32, 64:128].bitcast(BF16)
                                      ).then_inc(c_ctr, 1)
                        v3o = vpp[:, :].rearrange("p (c f) -> p c f", c=NK)
                        v3i = vbuf[:, :].rearrange("p (c f) -> p c f", c=NK)
                        v.tensor_scalar_mul(v3o,
                                            v3i[:, :, h * 128:h * 128 + 128],
                                            neglam[:, :]).then_inc(c_crow, 1)
                        augbuild(h, 0)
                        augbuild(h, 1)
                        v.wait_ge(c_ep4, h + 1)
                        if utb is not None and h == ATTN_PAIRS - 1:
                            v.tensor_copy(utb[:, :], attnT[:, :])
                        v.tensor_copy(rbc[:, :], psA[:, :])
                        v.drain()
                        v.tensor_mul(fmbuf[:, h * T:(h + 1) * T], attnT[:, :],
                                     rbc[:, :]).then_inc(c_epi, 1)
                blk.vector(ve)

                def ac(s):
                    for h in range(ATTN_PAIRS):
                        for u in range(64):
                            k, sh = u // 2, u % 2
                            ti, hd = k // 2, k % 2
                            r = u % 4
                            cc = hd * 16 + ti
                            s.wait_ge(c_max, h * 64 + 2 * k + 2)
                            s.activation(scrap[:, 0:1024], slots[r][:, :],
                                         AF.Exp, bias=nmx[:, cc:cc + 1],
                                         accum_out=sa2[:, 2 * cc + sh:
                                                       2 * cc + sh + 1]
                                         ).then_inc(s1sem[r], 1)
                        s.wait_ge(c_sumc, h + 1)
                        s.activation(lnS[:, :], sacc[:, :],
                                     AF.Ln).then_inc(c_lnS, 1)
                        for j in range(64):
                            kc, hd, qh = jdec(j)
                            s.wait_ge(c_peST, h * 64 + j + 1)
                            s.wait_ge(c_pv, max(0, h * 64 + j - 3))
                            s.wait_ge(c_epi, h)
                            s.activation(pthalf[j % 4][:, :],
                                         psA[:, qh * 1024:(qh + 1) * 1024],
                                         AF.Exp).then_inc(s2sem[qh], 1)
                        s.wait_ge(c_pv, h * 64 + 64)
                        s.activation(scrap[:, :], attnT[:, :],
                                     AF.Square).then_inc(c_ep1, 1)
                        s.wait_ge(c_ep2, h + 1)
                        s.activation(lnrow[0:1, :], psA[0:1, :], AF.Ln,
                                     bias=constb[0:1, 0:1],
                                     scale=1.0 / 128.0)
                        s.drain()
                        s.activation(rrow[0:1, :], lnrow[0:1, :], AF.Exp,
                                     bias=constb[0:1, 1:2],
                                     scale=-0.5).then_inc(c_ep3, 1)
                blk.scalar(ac)

            emit_tap("attnfm", fmbuf[:, :])
            emit_tap("sa", sacc[:, :])
            emit_tap("crow", crow2[0:1, :])
            emit_tap("crowlo", crow2[1:2, :])
            for _nm, _ap in (("nmx", nmx), ("lnS", lnS), ("chi", chi),
                             ("clo", clo), ("thi", thi), ("tlo", tlo),
                             ("cf", cf)):
                emit_tap(_nm, _ap[:, :])
            if utb is not None:
                emit_tap("ut", utb[:, :])
            emit_tap("ptb1", pt[3][:, :])
        S1.close()
        if stage < 3:
            return nc

        S2 = S.enter_context(ExitStack())
        acc = S2.enter_context(nc.sbuf_tensor("acc", [128, NT * DM], F32))
        postm_i = S2.enter_context(nc.sbuf_tensor("postm_i", [128, 256],
                                                  mybir.dt.int32))
        valstore = S2.enter_context(nc.sbuf_tensor("valstore", [128, 256],
                                                   F32))
        tokid_f = S2.enter_context(nc.sbuf_tensor("tokid_f", [128, 16], F32))
        tokid_i = S2.enter_context(nc.sbuf_tensor("tokid_i", [128, 16],
                                                  mybir.dt.int32))
        S2R = S.enter_context(ExitStack())
        x1fm = S2R.enter_context(
            nc.sbuf_tensor("x1fm", [128, ND * T], BF16, side="right"))

        def wsrc(e, k):
            if k < 2:
                if e < 16:
                    ap = (gwb_d if k == 0 else uwb_d)[e, :, :]
                else:
                    sh = shgb_d if k == 0 else shub_d
                    ap = sh[:, (e - 16) * DM:(e - 15) * DM]
            else:
                if e < 16:
                    ap = dwb_d[e, :, :]
                else:
                    ap = shdb_d[(e - 16) * DM:(e - 15) * DM, :]
            return ap.rearrange("(c p) f -> p c f", p=128)
        with ExitStack() as SD:
            sd_ = lambda name, shape, dt: SD.enter_context(
                nc.sbuf_tensor(name, list(shape), dt))
            scores_sb = sd_("scores_sb", (128, 256), F32)
            xst = [sd_(f"xst{i}", (128, DM), F32) for i in range(2)]
            x1b = [sd_(f"x1b{i}", (128, DM), BF16) for i in range(2)]
            pkw = sd_("pkw", (16, 4 * T), F32)
            selstore = sd_("selstore", (128, 256), F32)
            z16 = sd_("z16", (128, 16), F32)
            postm = sd_("postm", (128, 256), F32)
            zstage = sd_("zstage", (128, DM), F32)
            p2048 = sd_("p2048", (128, 96), F32)
            identf = sd_("identf", (128, 128), F32)
            eoffb = sd_("eoffb", (128, 1), F32)
            bn12 = sd_("bn12", (128, 12), F32)
            mv = sd_("mv", (128, 2), F32)
            rsq = sd_("rsq", (128, 1), F32)
            sb16 = sd_("sb16", (128, 16), F32)
            t16a = sd_("t16a", (128, 16), F32)
            t16b = sd_("t16b", (128, 16), F32)
            selb = sd_("selb", (128, 16), F32)
            emsk = sd_("emsk", (128, 16), F32)
            gs4 = sd_("gs4", (128, 4), F32)
            g4 = sd_("g4", (128, 4), F32)
            t4 = sd_("t4", (128, 4), F32)
            w2 = sd_("w2", (128, 2), F32)
            psO = [SD.enter_context(nc.psum_tensor(f"psO{i}", [128, DM], F32))
                   for i in range(2)]
            psT = [SD.enter_context(nc.psum_tensor(f"psT{i}", [128, DM], BF16))
                   for i in range(2)]
            psR = [SD.enter_context(nc.psum_tensor(f"psR{i}", [128, 128], F32))
                   for i in range(2)]
            with nc.Block() as blk:
                def sy(s):
                    def x1st(tj):
                        s.wait_ge(d_x1b, tj + 1)
                        s.dma_start(x1t_d[tj * 128:(tj + 1) * 128, :],
                                    x1b[tj % 2][:, :]
                                    ).then_inc(sp_x1d[tj % 2], 16)
                    for ti in range(NT):
                        if ti >= 2:
                            s.wait_ge(d_add, ti - 1)
                        s.dma_start(xst[ti % 2][:, :],
                                    x_d[ti * 128:(ti + 1) * 128, :]
                                    ).then_inc(d_xld[ti % 2], 16)
                        if ti >= 2:
                            x1st(ti - 2)
                    x1st(NT - 2)
                    x1st(NT - 1)
                blk.sync(sy)

                def gp(g):
                    g.dma_start(tokid_f[:, :], tokid_d[:, :]
                                ).then_inc(sp_dmisc, 16)
                    g.dma_start(identf[:, :], ident_d[:, :]
                                ).then_inc(sp_dmisc, 16)
                    g.dma_start(eoffb[:, :], eoff_d[:, :]
                                ).then_inc(sp_dmisc, 16)
                    g.wait_ge(sp_z, 3)
                    for k in range(4):
                        for ti in range(NT):
                            r0 = k * 4096 + ti * 128
                            if r0 + 128 <= 3 * 4096 + T:
                                g.dma_start(yrank_d[r0:r0 + 128, :],
                                            zstage[:, :]).then_inc(sp_zacc, 16)
                    g.dma_start(slotl_d[0:DUMP, :].rearrange(
                        "(p c) f -> p (c f)", c=96),
                        p2048[:, :]).then_inc(sp_zacc, 16)
                    g.dma_start(x1t_d[T:T + 1, :],
                                zstage[0:1, :]).then_inc(sp_zacc, 16)
                    g.dma_start(cmbd_d[T:T + 1, :],
                                zstage[0:1, 0:16]).then_inc(sp_cmb, 16)
                    for ti in range(NT):
                        g.wait_ge(r_dve, ti + 1)
                        g.dma_start(cmbd_d[ti * 128:(ti + 1) * 128, :],
                                    cmb[:, ti * 16:(ti + 1) * 16]
                                    ).then_inc(sp_cmb, 16)
                blk.gpsimd(gp)

                def pe(t):
                    def otile(ti):
                        if ti >= 2:
                            t.wait_ge(d_add, ti - 1)
                        last = None
                        for nch in range(2):
                            for jc in range(ND):
                                last = t.matmul(
                                    psO[ti % 2][:, nch * 512:(nch + 1) * 512],
                                    fmbuf[:, jc * T + ti * 128:
                                          jc * T + ti * 128 + 128],
                                    woT[:, jc * DM + nch * 512:
                                        jc * DM + (nch + 1) * 512],
                                    start=(jc == 0), stop=(jc == ND - 1))
                        last.then_inc(d_peO, 1)

                    def ttile(ti):
                        t.wait_ge(d_x1b, ti + 1)
                        if ti >= 2:
                            t.wait_ge(d_evT, ti - 1)
                        last = None
                        for dc in range(ND):
                            last = t.transpose(
                                psT[ti % 2][:, dc * 128:(dc + 1) * 128],
                                x1b[ti % 2][:, dc * 128:(dc + 1) * 128],
                                ident_b[:, :])
                        last.then_inc(d_peT, 1)

                    def rtile(ti):
                        t.wait_ge(d_evT, ti + 1)
                        if ti >= 2:
                            t.wait_ge(r_act, ti - 1)
                        last = None
                        for dc in range(ND):
                            last = t.matmul(
                                psR[ti % 2][:, 0:16],
                                x1fm[:, dc * T + ti * 128:
                                     dc * T + ti * 128 + 128],
                                routerT[:, dc * 16:(dc + 1) * 16],
                                start=(dc == 0), stop=(dc == ND - 1))
                        last.then_inc(r_pe, 1)

                    for ti in range(NT):
                        otile(ti)
                        if ti >= 2:
                            ttile(ti - 2)
                        if ti >= 4:
                            rtile(ti - 4)
                    ttile(NT - 2)
                    ttile(NT - 1)
                    for ti in range(NT - 4, NT):
                        rtile(ti)
                    t.wait_ge(r_act, NT)
                    t.wait_ge(sp_dmisc, 48)
                    for ti in range(NT):
                        t.wait_ge(r_dve, ti + 1)
                        if ti >= 2:
                            t.wait_ge(d_selE, ti - 1)
                        t.transpose(psR[ti % 2][0:16, 0:128],
                                    selstore[:, ti * 16:(ti + 1) * 16],
                                    identf[:, :]).then_inc(d_selT, 1)
                    t.wait_ge(d_pos, 1)
                    for tu in range(NT):
                        if tu >= 2:
                            t.wait_ge(d_posE, tu - 1)
                        t.transpose(psR[tu % 2][:, 0:16],
                                    pkw[0:16, 2 * T + tu * 128:
                                        2 * T + (tu + 1) * 128],
                                    identf[0:16, 0:16]).then_inc(d_posT, 1)
                blk.tensor(pe)

                def ve(v):
                    v.wait_ge(sp_dmisc, 48)
                    v.memset(pkw[0:16, 3 * T:4 * T], 0.0).then_inc(sp_z, 1)
                    v.memset(zstage[:, :], 0.0).then_inc(sp_z, 1)
                    v.memset(p2048[:, :], float(3 * 4096 + T)).then_inc(sp_z, 1)
                    v.memset(z16[:, :], 0.0)

                    def topk(ti):
                        v.wait_ge(r_act, ti + 1)
                        sc = scores_sb[:, ti * 16:(ti + 1) * 16]
                        s3 = sb16[:, :].rearrange("p (g c) -> p g c", g=4)
                        a3 = t16a[:, :].rearrange("p (g c) -> p g c", g=4)
                        b3 = t16b[:, :].rearrange("p (g c) -> p g c", g=4)
                        e3 = emsk[:, :].rearrange("p (g c) -> p g c", g=4)
                        gm1b = gs4[:, :].unsqueeze(2).broadcast_to([128, 4, 4])
                        g4b = g4[:, :].unsqueeze(2).broadcast_to([128, 4, 4])
                        v.tensor_add(sb16[:, :], sc, biasbc[:, :])
                        v.drain()
                        v.tensor_reduce(gs4[:, :], s3, axis=AX.X, op=OP.max)
                        v.drain()
                        v.tensor_tensor(a3, s3, gm1b, OP.is_equal)
                        v.drain()
                        v.scalar_tensor_tensor(t16b[:, :], t16a[:, :], -1e4,
                                               sb16[:, :], OP.mult, OP.add)
                        v.drain()
                        v.tensor_reduce(t4[:, :], b3, axis=AX.X, op=OP.max)
                        v.drain()
                        v.tensor_add(gs4[:, :], gs4[:, :], t4[:, :])
                        v.drain()
                        v.reduce_max(w2[:, 0:1], gs4[:, :], axis=AX.X)
                        v.drain()
                        v.tensor_scalar(g4[:, :], gs4[:, :], w2[:, 0:1], None,
                                        OP.is_equal)
                        v.drain()
                        v.scalar_tensor_tensor(g4[:, :], g4[:, :], -1e4,
                                               gs4[:, :], OP.mult, OP.add)
                        v.drain()
                        v.reduce_max(w2[:, 1:2], g4[:, :], axis=AX.X)
                        v.drain()
                        v.tensor_scalar(g4[:, :], gs4[:, :], w2[:, 1:2], None,
                                        OP.is_ge)
                        v.drain()
                        v.tensor_copy(e3, g4b)
                        v.tensor_scalar_add(t16a[:, :], sb16[:, :], 1.0)
                        v.drain()
                        v.tensor_mul(t16a[:, :], t16a[:, :], emsk[:, :])
                        v.drain()
                        v.tensor_scalar_add(t16a[:, :], t16a[:, :], -1.0)
                        v.memset(selb[:, :], 0.0)
                        v.drain()
                        for _ in range(4):
                            v.reduce_max(w2[:, 0:1], t16a[:, :], axis=AX.X)
                            v.drain()
                            v.tensor_scalar(t16b[:, :], t16a[:, :],
                                            w2[:, 0:1], None, OP.is_equal)
                            v.drain()
                            v.tensor_add(selb[:, :], selb[:, :], t16b[:, :])
                            v.scalar_tensor_tensor(t16a[:, :], t16b[:, :],
                                                   -1e5, t16a[:, :],
                                                   OP.mult, OP.add)
                            v.drain()
                        sst = selstore[:, ti * 16:(ti + 1) * 16]
                        v.tensor_scalar_min(sst, selb[:, :], 1.0)
                        v.drain()
                        v.tensor_tensor_scan(t16a[:, :], sst, z16[:, :],
                                             0.0, OP.add, OP.add)
                        v.drain()
                        v.tensor_sub(t16a[:, :], t16a[:, :], sst)
                        v.drain()
                        v.tensor_scalar(valstore[:, ti * 16:(ti + 1) * 16],
                                        t16a[:, :], 4096.0,
                                        tokid_f[:, ti:ti + 1],
                                        OP.mult, OP.add)
                        v.tensor_mul(t16b[:, :], sc, sst)
                        v.drain()
                        v.reduce_sum(w2[:, 0:1], t16b[:, :], axis=AX.X)
                        v.drain()
                        v.tensor_scalar_add(w2[:, 0:1], w2[:, 0:1], 1e-20)
                        v.drain()
                        v.reciprocal(w2[:, 1:2], w2[:, 0:1])
                        v.drain()
                        v.tensor_scalar(cmb[:, ti * 16:(ti + 1) * 16],
                                        t16b[:, :], w2[:, 1:2], 2.5,
                                        OP.mult, OP.mult).then_inc(r_dve, 1)

                    for ti in range(NT):
                        sl = slice(ti * DM, (ti + 1) * DM)
                        v.wait_ge(d_peO, ti + 1)
                        v.wait_ge(d_xld[ti % 2], 16 * (ti // 2 + 1))
                        v.tensor_add(acc[:, sl], psO[ti % 2][:, :],
                                     xst[ti % 2][:, :]).then_inc(d_add, 1)
                        v.bn_stats(bn12[:, 0:6], acc[:, ti * DM:ti * DM + 512])
                        v.bn_stats(bn12[:, 6:12],
                                   acc[:, ti * DM + 512:(ti + 1) * DM])
                        v.drain()
                        v.bn_aggr(mv[:, :], bn12[:, :]).then_inc(d_bn, 1)
                        v.wait_ge(d_act, ti + 1)
                        v.tensor_scalar(acc[:, sl], acc[:, sl], mv[:, 0:1],
                                        rsq[:, 0:1], OP.subtract, OP.mult)
                        if ti >= 2:
                            v.wait_ge(d_peT, ti - 1)
                        if ti >= 2:
                            v.wait_ge(sp_x1d[ti % 2], 16 * (ti // 2))
                        v.tensor_copy(x1b[ti % 2][:, :],
                                      acc[:, sl]).then_inc(d_x1b, 1)
                        if ti >= 2:
                            v.wait_ge(d_peT, ti - 1)
                            d3 = x1fm[:, :].rearrange("p (c f) -> p c f", c=ND)
                            v.tensor_copy(d3[:, :, (ti - 2) * 128:(ti - 2) * 128 + 128],
                                          psT[ti % 2][:, :].rearrange(
                                              "p (c f) -> p c f", c=ND)
                                          ).then_inc(d_evT, 1)
                        if ti >= 6:
                            topk(ti - 6)
                    for ti in (NT - 2, NT - 1):
                        v.wait_ge(d_peT, ti + 1)
                        d3 = x1fm[:, :].rearrange("p (c f) -> p c f", c=ND)
                        v.tensor_copy(d3[:, :, ti * 128:ti * 128 + 128],
                                      psT[ti % 2][:, :].rearrange(
                                          "p (c f) -> p c f", c=ND)
                                      ).then_inc(d_evT, 1)
                    for ti in range(NT - 6, NT):
                        topk(ti)
                    v.wait_ge(sp_dmisc, 48)
                    bsel = pkw[0:16, 0:T]
                    bcum = pkw[0:16, T:2 * T]
                    bpos = pkw[0:16, 2 * T:3 * T]
                    bzz = pkw[0:16, 3 * T:4 * T]
                    for ti in range(NT):
                        v.wait_ge(d_selT, ti + 1)
                        v.tensor_copy(pkw[0:16, ti * 128:(ti + 1) * 128],
                                      psR[ti % 2][0:16, 0:128]
                                      ).then_inc(d_selE, 1)
                    v.drain()
                    v.tensor_tensor_scan(bcum, bsel, bzz, 0.0,
                                         OP.add, OP.add)
                    v.drain()
                    v.tensor_scalar(bpos, bcum, -1.0, float(CAP - 1),
                                    OP.add, OP.min)
                    v.drain()
                    v.tensor_scalar(bcum, bpos, -63.5, 1.0 / 128.0,
                                    OP.add, OP.mult)
                    v.drain()
                    v.tensor_copy(pkw[0:16, T:2 * T].bitcast(mybir.dt.int32),
                                  bcum)
                    v.drain()
                    v.tensor_copy(bcum,
                                  pkw[0:16, T:2 * T].bitcast(mybir.dt.int32))
                    v.drain()
                    v.scalar_tensor_tensor(bpos, bcum, -128.0, bpos,
                                           OP.mult, OP.add)
                    v.drain()
                    v.scalar_tensor_tensor(bpos, bpos, 96.0, bcum,
                                           OP.mult, OP.add)
                    v.drain()
                    v.tensor_scalar(bpos, bpos, eoffb[0:16, 0:1],
                                    float(-DUMP), OP.add, OP.add)
                    v.drain()
                    v.tensor_mul(bpos, bpos, bsel)
                    v.drain()
                    v.tensor_scalar_add(bpos, bpos,
                                        float(DUMP)).then_inc(d_pos, 1)
                    for tu in range(NT):
                        v.wait_ge(d_posT, tu + 1)
                        v.tensor_copy(postm[:, tu * 16:(tu + 1) * 16],
                                      psR[tu % 2][:, 0:16]).then_inc(d_posE, 1)
                    v.drain()
                    v.tensor_copy(postm_i[:, :], postm[:, :])
                    v.tensor_copy(tokid_i[:, :], tokid_f[:, :]
                                  ).then_inc(d_pos, 1)
                blk.vector(ve)

                def ac(s):
                    def sig(ti):
                        s.wait_ge(r_pe, ti + 1)
                        s.activation(scores_sb[:, ti * 16:(ti + 1) * 16],
                                     psR[ti % 2][:, 0:16],
                                     AF.Sigmoid).then_inc(r_act, 1)

                    for ti in range(NT):
                        s.wait_ge(d_bn, ti + 1)
                        if ti >= 1:
                            s.wait_ge(d_x1b, ti)
                        s.activation(rsq[:, :], mv[:, 1:2], AF.Ln,
                                     bias=constb[:, 0:1])
                        s.drain()
                        s.activation(rsq[:, :], rsq[:, :], AF.Exp,
                                     scale=-0.5).then_inc(d_act, 1)
                        if ti >= 5:
                            sig(ti - 5)
                    for ti in range(NT - 5, NT):
                        sig(ti)
                blk.scalar(ac)
            emit_tap("postm", postm[:, :])
            emit_tap("selst", selstore[:, :])
        emit_tap("acc", acc[:, :])
        emit_tap("x1fm", x1fm[:, :])
        emit_tap("cmb", cmb[:, :])
        S2R.close()
        SR_.close()
        if stage < 4:
            return nc

        pes = []
        y0 = ry0 = sc0 = sl0 = xg0 = wg0 = tr0 = ro0 = 0
        for i in range(24):
            routed = i >= 8
            e = (i - 8) if routed else 16 + i // 4
            q = 0 if routed else i % 4
            nt = 6 if routed else 4
            rec = dict(i=i, routed=routed, e=e, q=q, ro=ro0,
                       cap=CAP if routed else 512, nt=nt,
                       mg=3 * e, mu=3 * e + 1, md=3 * e + 2,
                       ybase=y0, rybase=ry0, scat0=sc0, slotl0=sl0,
                       xg0=xg0, wg0=wg0, tr0=tr0)
            lastq = (not routed) and q == 3
            rec["wret_g"] = 2 if (routed or lastq) else 0
            rec["wret_y"] = 1 if (routed or lastq) else 0
            pes.append(rec)
            y0 += nt
            ry0 += nt if routed else 0
            sc0 += 16 if routed else 0
            sl0 += 1 if routed else 0
            xg0 += nt
            wg0 += nt if routed else 0
            tr0 += nt
            ro0 += 1 if routed else 0
        RYT = ry0
        LORD = [48, 49, 50, 51, 52, 53] + list(range(48))
        POS = {m: p for p, m in enumerate(LORD)}
        WTHR = {}
        for m in range(54):
            if m >= 48:
                WTHR[m] = 32 * (min(m, 50) - 47)
            else:
                WTHR[m] = 96 + 16 * (m + 1)
        for m in (51, 52, 53):
            WTHR[m] = 32 * (m - 50)

        with ExitStack() as SE:
            se_ = lambda name, shape, dt: SE.enter_context(
                nc.sbuf_tensor(name, list(shape), dt))
            ge_buf = se_("ge_buf", (128, ND * CAP), BF16)
            hs = [se_(f"hs{i}", (128, CAP), F32) for i in range(2)]
            wsl = [se_(f"wsl{si}", (128, ND * DM), BF16) for si in range(4)]
            x1e_fm = [se_(f"x1efm{i}", (128, ND * CAP), BF16)
                      for i in range(2)]
            x1e_tm = se_("x1etm", (128, 6 * DM), BF16)
            ystage = [se_(f"ystage{i}", (128, DM), F32) for i in range(4)]
            slotl_sb = [se_(f"slotl{i}", (128, 6), F32) for i in range(4)]
            slotli = [se_(f"slotli{i}", (128, 6), mybir.dt.int32)
                      for i in range(4)]
            widx = [se_(f"widx{i}", (128, 6), mybir.dt.int32)
                    for i in range(4)]
            ydst = [se_(f"ydst{i}", (128, 6), mybir.dt.int32)
                    for i in range(4)]
            tmpf = se_("tmpf", (128, 6), F32)
            tmpk = se_("tmpk", (128, 6), F32)
            wg_sb = [se_(f"wg{i}", (128, 6), F32) for i in range(4)]
            psH = SE.enter_context(nc.psum_tensor("psH", [128, DM], F32))
            psU = SE.enter_context(nc.psum_tensor("psU", [128, DM], F32))
            psY = [SE.enter_context(nc.psum_tensor(f"psY{i}", [128, DM], F32))
                   for i in range(2)]

            def wslot(m):
                return wsl[POS[m] % 4], POS[m] // 4 + 1

            with nc.Block() as blk:
                def gp(g):
                    bregs = []

                    firstA = [True]

                    def prepA(j):
                        if not (0 <= j < 24 and pes[j]["routed"]):
                            return
                        if firstA[0]:
                            firstA[0] = False
                            g.wait_ge(sp_zacc, 16 * 66)
                            g.wait_ge(sp_dmisc, 48)
                        e_ = pes[j]["e"]
                        for tau in range(16):
                            col = tau * 16 + e_
                            g.indirect_dma_start(
                                slotl_d[:, 0:1],
                                IndirectOffsetOnAxis(
                                    postm_i[:, col:col + 1], 0),
                                valstore[:, col:col + 1], None,
                                bounds_check=bregs[0], oob_is_err=False
                                ).then_inc(sp_scat, 16)

                    def slotload(j):
                        if not (0 <= j < 24 and pes[j]["routed"]):
                            return
                        e_ = pes[j]["e"]
                        g.wait_ge(sp_scat, 16 * (pes[j]["scat0"] + 16))
                        g.dma_start(
                            slotl_sb[j % 4][:, :],
                            slotl_d[0:DUMP, :].rearrange(
                                "(p c) f -> p (c f)", c=96)
                            [:, e_ * 6:(e_ + 1) * 6]
                            ).then_inc(sp_slotl, 16)

                    def prepB(j):
                        rec = pes[j]
                        if j == 0:
                            g.wait_ge(sp_cmb, 16 * 17)
                            g.wait_ge(sp_x1d[0], 16 * 8)
                            g.wait_ge(sp_x1d[1], 16 * 8)
                        if rec["routed"]:
                            e_ = rec["e"]
                            g.wait_ge(d_slotc, rec["ro"] + 1)
                            g.wait_ge(sp_xts, 16 * rec["tr0"])
                            for r in range(6):
                                g.indirect_dma_start(
                                    x1e_tm[:, r * DM:(r + 1) * DM], None,
                                    x1t_d[:, :],
                                    IndirectOffsetOnAxis(
                                        slotli[j % 4][:, r:r + 1], 0)
                                    ).then_inc(sp_xg, 16)
                            cmbflat = cmbd_d[:, :].rearrange(
                                "a b -> (a b)").unsqueeze(1)
                            for r in range(6):
                                g.indirect_dma_start(
                                    wg_sb[j % 4][:, r:r + 1], None,
                                    cmbflat,
                                    IndirectOffsetOnAxis(
                                        widx[j % 4][:, r:r + 1], 0)
                                    ).then_inc(sp_wg, 16)
                        else:
                            g.wait_ge(sp_xts, 16 * rec["tr0"])
                            for r in range(4):
                                row0 = (rec["q"] * 4 + r) * 128
                                g.dma_start(x1e_tm[:, r * DM:(r + 1) * DM],
                                            x1t_d[row0:row0 + 128, :]
                                            ).then_inc(sp_xg, 16)

                    def sadds(i):
                        rec = pes[i]
                        if not rec["routed"]:
                            return
                        for r in range(rec["nt"]):
                            ridx = rec["rybase"] + r
                            g.wait_ge(e_yev, rec["ybase"] + r + 1)
                            g.indirect_dma_start(
                                yrank_d[:, :],
                                IndirectOffsetOnAxis(
                                    ydst[i % 4][:, r:r + 1], 0),
                                ystage[ridx % 4][:, :], None,
                                bounds_check=bregs[1], oob_is_err=False
                                ).then_inc(sp_sadd, 16)

                    with g.register() as b0, g.register() as b1:
                        bregs.append(b0)
                        bregs.append(b1)
                        g.reg_mov(b0, DUMP - 1)
                        g.reg_mov(b1, 3 * 4096 + T - 1)
                        prepB(0)
                        prepB(1)
                        for i in range(24):
                            prepA(i + 4)
                            slotload(i + 3)
                            if i + 2 < 24:
                                prepB(i + 2)
                            sadds(i)
                blk.gpsimd(gp)

                def pe(t):
                    for rec in pes:
                        i = rec["i"]
                        cap = rec["cap"]
                        slg, geng = wslot(rec["mg"])
                        slu, genu = wslot(rec["mu"])
                        sld, gend = wslot(rec["md"])
                        t.wait_ge(e_wld[POS[rec["mg"]] % 4], 16 * geng)
                        t.wait_ge(e_wld[POS[rec["mu"]] % 4], 16 * genu)
                        t.wait_ge(sp_xts, 16 * (rec["tr0"] + rec["nt"]))
                        fmb = x1e_fm[i % 2]
                        nchl = ([(0, 512), (512, 256)] if cap == CAP
                                else [(0, 512)])
                        for fc in range(ND):
                            n = i * ND + fc
                            for k, ps, semw, sl_ in ((0, psH, e_silu, slg),
                                                     (1, psU, e_gmul, slu)):
                                t.wait_ge(semw, n)
                                last = None
                                for off, w in nchl:
                                    for dc in range(ND):
                                        last = t.matmul(
                                            ps[:, off:off + w],
                                            sl_[:, dc * DM + fc * 128:
                                                dc * DM + fc * 128 + 128],
                                            fmb[:, dc * CAP + off:
                                                dc * CAP + off + w],
                                            start=(dc == 0),
                                            stop=(dc == ND - 1))
                                last.then_inc(e_peH if k == 0 else e_peU, 1)
                        t.nop().then_inc(sp_gdone, 1)
                        if rec["wret_g"]:
                            t.nop().then_inc(e_wdone, rec["wret_g"])
                        t.wait_ge(e_gmul, (i + 1) * ND)
                        t.wait_ge(e_wld[POS[rec["md"]] % 4], 16 * gend)
                        for r in range(rec["nt"]):
                            yidx = rec["ybase"] + r
                            t.wait_ge(e_yev, max(0, yidx - 1))
                            last = None
                            for nch in range(2):
                                for fc in range(ND):
                                    last = t.matmul(
                                        psY[yidx % 2][:, nch * 512:
                                                      (nch + 1) * 512],
                                        ge_buf[:, fc * CAP + r * 128:
                                               fc * CAP + r * 128 + 128],
                                        sld[:, fc * DM + nch * 512:
                                            fc * DM + (nch + 1) * 512],
                                        start=(fc == 0), stop=(fc == ND - 1))
                            last.then_inc(e_peY, 1)
                        if rec["wret_y"]:
                            t.nop().then_inc(e_wdone, rec["wret_y"])
                blk.tensor(pe)

                def ac(s):
                    emitted = set()

                    def load(m):
                        if not (0 <= m < 54) or m in emitted:
                            return
                        emitted.add(m)
                        e_, k = divmod(m, 3)
                        s.wait_ge(e_wdone, max(0, POS[m] - 3))
                        s.wait_ge(sp_wcast, WTHR[m])
                        sl_, _ = wslot(m)
                        d3 = sl_[:, :].rearrange("p (c f) -> p c f", c=ND)
                        s.dma_start(d3, wsrc(e_, k)
                                    ).then_inc(e_wld[POS[m] % 4], 16)

                    for m in LORD[0:4]:
                        load(m)
                    for rec in pes:
                        cap = rec["cap"]
                        j = rec["i"]
                        for fc in range(ND):
                            n = j * ND + fc
                            s.wait_ge(e_peH, n + 1)
                            s.wait_ge(e_gmul, max(0, n - 1))
                            s.activation(hs[n % 2][:, 0:cap], psH[:, 0:cap],
                                         AF.Silu).then_inc(e_silu, 1)
                        if j + 1 < 24:
                            load(3 * pes[j + 1]["e"] + 1)
                            load(3 * pes[j + 1]["e"] + 2)
                        if j + 2 < 24:
                            load(3 * pes[j + 2]["e"])
                    assert len(emitted) == 54, len(emitted)
                blk.scalar(ac)

                def ve(v):
                    def slotcast(j):
                        if not (0 <= j < 24 and pes[j]["routed"]):
                            return
                        if pes[j]["ro"] >= 4:
                            v.wait_ge(sp_sadd, 16 * (pes[j - 4]["rybase"]
                                                     + pes[j - 4]["nt"]))
                        v.wait_ge(sp_slotl, 16 * (pes[j]["slotl0"] + 1))
                        vv = slotl_sb[j % 4][:, :]
                        v.tensor_copy(ydst[j % 4][:, :], vv)
                        v.tensor_scalar(tmpk[:, :], vv, -2047.5,
                                        1.0 / 4096.0, OP.add, OP.mult)
                        v.drain()
                        v.tensor_copy(tmpk[:, :].bitcast(mybir.dt.int32),
                                      tmpk[:, :])
                        v.drain()
                        v.tensor_copy(tmpk[:, :],
                                      tmpk[:, :].bitcast(mybir.dt.int32))
                        v.drain()
                        v.scalar_tensor_tensor(tmpf[:, :], tmpk[:, :],
                                               -4096.0, vv,
                                               OP.mult, OP.add)
                        v.drain()
                        v.tensor_copy(slotli[j % 4][:, :], tmpf[:, :])
                        v.tensor_scalar(tmpk[:, :], tmpf[:, :],
                                        16.0, float(pes[j]["e"]),
                                        OP.mult, OP.add)
                        v.drain()
                        v.tensor_copy(widx[j % 4][:, :],
                                      tmpk[:, :]).then_inc(d_slotc, 1)

                    slotcast(0)
                    slotcast(1)
                    for rec in pes:
                        i = rec["i"]
                        cap = rec["cap"]
                        for fc in range(ND):
                            n = i * ND + fc
                            v.wait_ge(e_silu, n + 1)
                            v.wait_ge(e_peU, n + 1)
                            v.tensor_mul(ge_buf[:, fc * CAP:fc * CAP + cap],
                                         hs[n % 2][:, 0:cap], psU[:, 0:cap]
                                         ).then_inc(e_gmul, 1)
                        slotcast(i + 3)
                        if rec["routed"]:
                            v.wait_ge(sp_wg, 16 * (rec["wg0"] + rec["nt"]))
                        for r in range(rec["nt"]):
                            yidx = rec["ybase"] + r
                            v.wait_ge(e_peY, yidx + 1)
                            if rec["routed"]:
                                ridx = rec["rybase"] + r
                                if ridx >= 4:
                                    v.wait_ge(sp_sadd, 16 * (ridx - 3))
                                v.tensor_scalar(ystage[ridx % 4][:, :],
                                                psY[yidx % 2][:, :],
                                                wg_sb[i % 4][:, r:r + 1],
                                                None, OP.mult
                                                ).then_inc(e_yev, 1)
                            else:
                                tok = rec["q"] * 4 + r
                                sl_t = slice(tok * DM, (tok + 1) * DM)
                                v.scalar_tensor_tensor(
                                    acc[:, sl_t], psY[yidx % 2][:, :], 1.0,
                                    acc[:, sl_t], OP.mult, OP.add
                                    ).then_inc(e_yev, 1)
                blk.vector(ve)

                def sy(s):
                    cumx = 0
                    cumt = 0
                    for rec in pes:
                        i = rec["i"]
                        cumx += rec["nt"]
                        s.wait_ge(sp_xg, 16 * cumx)
                        if i >= 2:
                            s.wait_ge(sp_gdone, i - 1)
                        fm3 = x1e_fm[i % 2][:, :].rearrange(
                            "p (c f) -> p c f", c=ND)
                        for r in range(rec["nt"]):
                            s.dma_start_transpose(
                                fm3[:, :, r * 128:(r + 1) * 128],
                                x1e_tm[:, r * DM:(r + 1) * DM]
                                ).then_inc(sp_xts, 16)
                            cumt += 1
                            s.wait_ge(sp_xts, 16 * cumt)
                blk.sync(sy)
            emit_tap("g0", ge_buf[:, :])
        emit_tap("slotl", slotl_d[:, :])

        with ExitStack() as SF:
            sf_ = lambda name, shape, dt: SF.enter_context(
                nc.sbuf_tensor(name, list(shape), dt))
            ost = [sf_(f"ost{i}", (128, DM), F32) for i in range(2)]
            accr = [sf_(f"accr{i}", (128, 4 * DM), F32) for i in range(2)]
            bn12f = sf_("bn12f", (128, 12), F32)
            mvf = sf_("mvf", (128, 2), F32)
            rsqf = sf_("rsqf", (128, 1), F32)
            with nc.Block() as blk:
                def ve(v):
                    for ti in range(NT):
                        sl = slice(ti * DM, (ti + 1) * DM)
                        v.wait_ge(f_ld[ti % 2], 16 * 4 * (ti // 2 + 1))
                        for k in range(4):
                            v.tensor_add(acc[:, sl], acc[:, sl],
                                         accr[ti % 2][:, k * DM:(k + 1) * DM])
                            v.drain()
                        v.nop().then_inc(f_addc, 1)
                        v.bn_stats(bn12f[:, 0:6], acc[:, ti * DM:ti * DM + 512])
                        v.bn_stats(bn12f[:, 6:12],
                                   acc[:, ti * DM + 512:(ti + 1) * DM])
                        v.drain()
                        v.bn_aggr(mvf[:, :], bn12f[:, :]).then_inc(f_dve, 1)
                        v.wait_ge(f_act, ti + 1)
                        if ti >= 2:
                            v.wait_ge(f_out[ti % 2], 16 * (ti // 2))
                        v.tensor_scalar(ost[ti % 2][:, :], acc[:, sl],
                                        mvf[:, 0:1], rsqf[:, 0:1],
                                        OP.subtract, OP.mult).then_inc(f_dve, 1)
                blk.vector(ve)

                def ac(s):
                    for ti in range(NT):
                        s.wait_ge(f_dve, 2 * ti + 1)
                        s.activation(rsqf[:, :], mvf[:, 1:2], AF.Ln,
                                     bias=constb[:, 0:1])
                        s.drain()
                        s.activation(rsqf[:, :], rsqf[:, :], AF.Exp,
                                     scale=-0.5).then_inc(f_act, 1)
                blk.scalar(ac)

                def sy(s):
                    def fload(ti):
                        if ti >= 2:
                            s.wait_ge(f_addc, ti - 1)
                        for k in range(4):
                            r0 = k * 4096 + ti * 128
                            s.dma_start(accr[ti % 2][:, k * DM:(k + 1) * DM],
                                        yrank_d[r0:r0 + 128, :]
                                        ).then_inc(f_ld[ti % 2], 16)
                    s.wait_ge(sp_sadd, 16 * RYT)
                    fload(0)
                    fload(1)
                    for ti in range(NT):
                        if ti + 2 < NT:
                            fload(ti + 2)
                        s.wait_ge(f_dve, 2 * ti + 2)
                        s.dma_start(out_d[ti * 128:(ti + 1) * 128, :],
                                    ost[ti % 2][:, :]).then_inc(f_out[ti % 2], 16)
                    s.wait_ge(f_out[0], 16 * 8)
                    s.wait_ge(f_out[1], 16 * 8)
                blk.sync(sy)

    return nc


_CONSTS = None


def kernel(L0_attn_x, L0_attn_Wq, L0_attn_Wk, L0_attn_Wv, L0_attn_Wo,
           L0_attn_lambda_q1, L0_attn_lambda_k1, L0_attn_lambda_q2,
           L0_attn_lambda_k2, L0_ffn_router_w, L0_ffn_bias, L0_ffn_gate_w,
           L0_ffn_up_w, L0_ffn_down_w, L0_ffn_sh_gate_w, L0_ffn_sh_up_w,
           L0_ffn_sh_down_w):
    global _CONSTS
    if _CONSTS is None:
        _CONSTS = _host_consts()
    c = _CONSTS
    x = np.ascontiguousarray(np.asarray(L0_attn_x, dtype=np.float32))
    common = {
        "wq": np.ascontiguousarray(L0_attn_Wq, dtype=np.float32),
        "wk": np.ascontiguousarray(L0_attn_Wk, dtype=np.float32),
        "wv": np.ascontiguousarray(L0_attn_Wv, dtype=np.float32),
        "wo": np.ascontiguousarray(L0_attn_Wo, dtype=np.float32),
        "lam0": np.ascontiguousarray(L0_attn_lambda_q1, dtype=np.float32),
        "lam1": np.ascontiguousarray(L0_attn_lambda_k1, dtype=np.float32),
        "lam2": np.ascontiguousarray(L0_attn_lambda_q2, dtype=np.float32),
        "lam3": np.ascontiguousarray(L0_attn_lambda_k2, dtype=np.float32),
        "rw": np.ascontiguousarray(L0_ffn_router_w, dtype=np.float32),
        "rbias": np.ascontiguousarray(L0_ffn_bias, dtype=np.float32),
        "gw": np.ascontiguousarray(L0_ffn_gate_w, dtype=np.float32),
        "uw": np.ascontiguousarray(L0_ffn_up_w, dtype=np.float32),
        "dw": np.ascontiguousarray(L0_ffn_down_w, dtype=np.float32),
        "shg": np.ascontiguousarray(L0_ffn_sh_gate_w, dtype=np.float32),
        "shu": np.ascontiguousarray(L0_ffn_sh_up_w, dtype=np.float32),
        "shd": np.ascontiguousarray(L0_ffn_sh_down_w, dtype=np.float32),
        "aq": c["aq"], "bq": c["bq"], "ak": c["ak"], "bk": c["bk"],
        "perm": c["permT"], "ident": c["ident"],
        "tokid": c["tokid"], "eoff": c["eoff"],
    }
    nc = build_program(stage=4)
    in_maps = [dict(common, x=np.ascontiguousarray(x[b])) for b in range(8)]
    res = run_bass_kernel_spmd(nc, in_maps, core_ids=list(range(8)))
    out = np.stack([np.asarray(r["out"]) for r in res.results], axis=0)
    return out.astype(np.float32)
